# version 05
# speedup vs baseline: 1.1113x; 1.0470x over previous
import contextlib
import math
import numpy as np
import ml_dtypes
import concourse.bass as bass
import concourse.mybir as mybir
from concourse.bass_utils import run_bass_kernel_spmd

F32 = mybir.dt.float32
BF16 = mybir.dt.bfloat16
AF = mybir.ActivationFunctionType
ALU = mybir.AluOpType
AX = mybir.AxisListType
BF = ml_dtypes.bfloat16

T = 4096
D = 1024
P1B_SERIAL = False
DRAIN_INC = False
P1B_DELAY = False
P1B_DBG = False
P1C_LIMIT = 64
P1C_FEAT = "all"
P1C_LEVEL = 10
P1C_HEADS = "all"
P1C_SDBG = False
NCORE = 8
NEG = -1e30

ENGINES = ("sync", "gpsimd", "tensor", "vector", "scalar")


class _Op:
    __slots__ = ("eng", "fn", "reads", "writes", "dma", "mark", "idx", "deps",
                 "level", "tag", "semname", "prewait")

    def __init__(self, eng, fn, reads, writes, dma, idx, tag):
        self.eng = eng
        self.fn = fn
        self.reads = reads
        self.writes = writes
        self.dma = dma
        self.idx = idx
        self.tag = tag
        self.mark = False
        self.deps = ()
        self.level = None


def _sem_of(op):
    return ("q" + op.eng) if op.dma else ("p" + op.eng)


class Prog:

    def __init__(self, nc):
        self.nc = nc
        self.ops = []
        self.buf = {}
        self.es = contextlib.ExitStack()
        self.phase_es = None
        self.sb_handles = {}
        self.fence = {}
        self.last_per_eng = {}
        self.last_dmas = {}

    def sb(self, name, shape, dtype):
        es = self.phase_es if self.phase_es is not None else self.es
        h = es.enter_context(self.nc.sbuf_tensor(name, shape, dtype))
        self.sb_handles[name] = h
        return h

    def ps(self, name, shape, dtype=F32):
        es = self.phase_es if self.phase_es is not None else self.es
        return es.enter_context(self.nc.psum_tensor(name, shape, dtype))

    def sb_in(self, es, name, shape, dtype):
        return es.enter_context(self.nc.sbuf_tensor(name, shape, dtype))

    def ps_in(self, es, name, shape, dtype=F32):
        return es.enter_context(self.nc.psum_tensor(name, shape, dtype))

    def barrier(self):
        f = dict(self.last_per_eng)
        for q, lst in self.last_dmas.items():
            for j, oid in enumerate(lst):
                f[(q, "dma", j)] = oid
        self.fence = f

    def op(self, eng, fn, reads=(), writes=(), dma=False, tag=""):
        o = _Op(eng, fn, tuple(reads), tuple(writes), dma, len(self.ops), tag)
        deps = set(self.fence.values())
        for r in o.reads:
            st = self.buf.get(r)
            if st is not None and st[0] is not None:
                deps.add(st[0])
        for w in o.writes:
            st = self.buf.get(w)
            if st is not None:
                if st[0] is not None:
                    deps.add(st[0])
                deps.update(st[1])
        o.deps = tuple(sorted(deps))
        real = []
        for d in o.deps:
            dop = self.ops[d]
            if (dop.eng == o.eng and not dop.dma and not o.dma
                    and o.eng == "tensor"):
                continue
            real.append(d)
            dop.mark = True
        o.deps = tuple(real)
        if dma:
            o.mark = True
        self.ops.append(o)
        self.last_per_eng[(eng, dma)] = o.idx
        if dma:
            lst = self.last_dmas.setdefault("q" + eng, [])
            lst.append(o.idx)
            if len(lst) > self.NSEM:
                lst.pop(0)
        for r in o.reads:
            self.buf.setdefault(r, [None, []])[1].append(o.idx)
        for w in o.writes:
            self.buf[w] = [o.idx, []]
        return o

    NSEM = 8

    def finalize(self):
        counts = {}
        qpos = {}
        for o in self.ops:
            if o.dma:
                q = "q" + o.eng
                j = qpos.get(q, 0)
                qpos[q] = j + 1
                s = f"{q}_{j % self.NSEM}"
                o.semname = s
                o.prewait = (s, 16 * (j // self.NSEM))
                o.level = 16 * (j // self.NSEM + 1)
                counts[s] = o.level
                o.mark = True
            elif o.mark:
                s = "p" + o.eng
                o.semname = s
                o.prewait = None
                counts[s] = counts.get(s, 0) + 1
                o.level = counts[s]
        sems = {s: self.es.enter_context(self.nc.semaphore(s))
                for s in counts}
        final_counts = dict(counts)

        per_eng = {e: [] for e in ENGINES}
        for o in self.ops:
            per_eng[o.eng].append(o)

        blk = self.es.enter_context(self.nc.Block())
        ops_map = self.ops

        def body(engh, eng_name):
            hw = {}
            for o in per_eng[eng_name]:
                for d in o.deps:
                    dop = ops_map[d]
                    s = dop.semname
                    lvl = dop.level
                    if hw.get(s, 0) < lvl:
                        engh.wait_ge(sems[s], lvl)
                        hw[s] = lvl
                if o.dma and o.prewait and o.prewait[1] > 0:
                    s, lvl = o.prewait
                    if hw.get(s, 0) < lvl:
                        engh.wait_ge(sems[s], lvl)
                        hw[s] = lvl
                instr = o.fn(engh)
                if o.mark:
                    if instr is None:
                        raise RuntimeError(f"marked op returned no instr: {o.tag}")
                    instr.then_inc(sems[o.semname], 16 if o.dma else 1)
            if eng_name == "sync":
                for s, v in final_counts.items():
                    engh.wait_ge(sems[s], v)

        blk.sync(lambda e: body(e, "sync"))
        for nm in ("gpsimd", "tensor", "vector", "scalar"):
            if per_eng[nm]:
                getattr(blk, nm)(lambda e, _n=nm: body(e, _n))
        self.es.close()


_CONST_CACHE = {}


def host_consts():
    if _CONST_CACHE:
        return _CONST_CACHE
    c = {}
    c["c_onescol"] = np.ones((128, 1), np.float32)
    c["c_ones512"] = np.ones((1, 512), np.float32)
    c["c_ones1"] = np.ones((1, 128), np.float32)
    c["c_id128"] = np.eye(128, dtype=BF)
    c["c_id128f"] = np.eye(128, dtype=np.float32)
    bo = np.zeros((128, 2), np.float32)
    bo[:64, 0] = 1.0
    bo[64:, 1] = 1.0
    c["c_blockones"] = bo
    s2 = np.zeros((2, 128), np.float32)
    s2[0, :64] = 1.0
    s2[1, 64:] = 1.0
    c["c_sel2"] = s2
    s16 = np.zeros((16, 1024), np.float32)
    for h in range(16):
        s16[h, 64 * h:64 * h + 64] = 1.0
    c["c_sel16"] = s16
    sc = np.zeros((16, 8 * 128), np.float32)
    for cc in range(8):
        sc[2 * cc, 128 * cc:128 * cc + 64] = 1.0
        sc[2 * cc + 1, 128 * cc + 64:128 * cc + 128] = 1.0
    c["c_selc"] = sc
    oh = np.zeros((8, 8 * 128), np.float32)
    for e in range(8):
        oh[e, 128 * e:128 * e + 128] = 1.0
    c["c_oneh"] = oh
    R = np.zeros((64, 64), np.float32)
    for i in range(32):
        R[i, i + 32] = -1.0
        R[i + 32, i] = 1.0
    RT = np.zeros((128, 128), np.float32)
    RT[:64, :64] = R.T
    RT[64:, 64:] = R.T
    c["c_rot"] = RT.astype(BF)
    m01 = np.zeros((64, 64), np.float32)
    for k in range(64):
        m01[k, k + 1:] = 1.0
    c["c_mask01"] = np.tile(m01, (1, 16)).astype(BF)
    masks = np.zeros((8, 128, 1024), np.float32)
    for w0 in range(2):
        for pp in range(4):
            i = (pp * 128 + np.arange(128))[:, None]
            w = 0 if w0 else 1
            tq = 512 * w + pp * 128 + np.arange(128)[:, None]
            jg = 512 * (w - 1) + np.arange(1024)[None, :]
            valid = (jg >= 0) & (jg <= tq) & (tq <= jg + 512)
            masks[w0 * 4 + pp] = np.where(valid, 0.0, NEG)
    c["c_attnmask"] = masks.reshape(8 * 128, 1024).astype(BF)
    DIM_HEAD, WINDOW, QK_SCALE = 64, 512, 8.0
    half_idx = np.arange(0, DIM_HEAD, 2, dtype=np.float64)
    inv_freq = 1.0 / 10000.0 ** (half_idx / DIM_HEAD)
    t = np.arange(T, dtype=np.float64)
    freqs = t[:, None] * inv_freq[None, :]
    freqs = np.concatenate([freqs, freqs], -1)
    base = (half_idx + 0.4 * DIM_HEAD) / (1.4 * DIM_HEAD)
    power = (t - T // 2) / float(WINDOW // 2)
    scale = base[None, :] ** power[:, None]
    scale = np.concatenate([scale, scale], -1)
    cs, sn = np.cos(freqs), np.sin(freqs)
    csq = (cs * scale * QK_SCALE).T
    ssq = (sn * scale * QK_SCALE).T
    csk = (cs / scale).T
    ssk = (sn / scale).T
    for nm, tab in (("c_csq", csq), ("c_ssq", ssq), ("c_csk", csk), ("c_ssk", ssk)):
        c[nm] = np.tile(tab.astype(np.float32), (2, 1)).astype(BF)
    _CONST_CACHE.update(c)
    return c


def _pm(v):
    v = np.asarray(v, np.float32)
    n = v.shape[0] // 128
    return np.ascontiguousarray(v.reshape(n, 128).T)


def host_weights(inputs):
    w = {}
    g = lambda k: np.asarray(inputs[k], np.float32)
    w["WwT"] = np.ascontiguousarray(g("L0_attn_W_write").T)
    w["WgT"] = np.ascontiguousarray(g("L0_attn_W_gate").T)
    w["WbT"] = np.ascontiguousarray(g("L0_attn_W_beta").T)
    w["WaT"] = np.ascontiguousarray(g("L0_attn_W_alpha").T)
    w["WoutT"] = np.ascontiguousarray(g("L0_attn_W_out").T)
    w["A_log"] = g("L0_attn_A_log").reshape(16, 1)
    w["dt_bias"] = g("L0_attn_dt_bias").reshape(16, 1)
    for l in range(2):
        w[f"qkv{l}"] = np.ascontiguousarray(g("L0_ffn_wqkv")[l])
        w[f"wo{l}"] = np.ascontiguousarray(g("L0_ffn_wo")[l])
        w1 = g("L0_ffn_w1")[l]
        w1p = np.zeros((1024, 5632), np.float32)
        w1p[:, :2730] = w1[:, :2730]
        w1p[:, 2816:2816 + 2730] = w1[:, 2730:]
        w[f"w1{l}"] = w1p
        b1 = g("L0_ffn_b1")[l]
        b1p = np.zeros(5632, np.float32)
        b1p[:2730] = b1[:2730]
        b1p[2816:2816 + 2730] = b1[2730:]
        w[f"b1a{l}"] = _pm(b1p[:2816])
        w[f"b1g{l}"] = _pm(b1p[2816:])
        w2 = g("L0_ffn_w2")[l]
        w2p = np.zeros((2816, 1024), np.float32)
        w2p[:2730] = w2
        w[f"w2{l}"] = w2p
        w[f"b2{l}"] = _pm(g("L0_ffn_b2")[l])
        w[f"lng{l}"] = _pm(g("L0_ffn_ln_g")[l])
        w[f"lnb{l}"] = _pm(g("L0_ffn_ln_b")[l])
        w[f"lngT{l}"] = np.stack([g("L0_ffn_ln_g")[l], g("L0_ffn_ln_b")[l]])
        qs = g("L0_ffn_q_scale")[l]
        ks = g("L0_ffn_k_scale")[l]
        w[f"qs{l}"] = np.tile(qs, 2).reshape(128, 1).copy()
        w[f"ks{l}"] = np.tile(ks, 2).reshape(128, 1).copy()
    w["Wq"] = g("L1_attn_Wq")
    w["Wk"] = g("L1_attn_Wk")
    w["Wt"] = g("L1_attn_Wt")
    w["Wqa"] = g("L1_attn_Wqa")
    w["Wka"] = g("L1_attn_Wka")
    w["bq"] = _pm(g("L1_attn_bq"))
    w["bk"] = _pm(g("L1_attn_bk"))
    w["bt"] = _pm(g("L1_attn_bt"))
    w["bqa"] = g("L1_attn_bqa").reshape(16, 1)
    w["bka"] = g("L1_attn_bka").reshape(16, 1)
    w["gwT"] = np.ascontiguousarray(g("L1_ffn_gate_w").T)
    w["wg"] = g("L1_ffn_wg")
    w["wu"] = g("L1_ffn_wu")
    w["wd"] = g("L1_ffn_wd")
    w["swg"] = g("L1_ffn_swg")
    w["swu"] = g("L1_ffn_swu")
    w["swd"] = g("L1_ffn_swd")
    return w


def _xtile_hbm(ap, c0, nch, t0, tn, ncols=T, pad=0):
    return ap[c0 * 128:(c0 + nch) * 128, pad + t0: pad + t0 + tn].rearrange(
        "(c p) t -> p c t", p=128)


def emit_p0(p, d):
    nc = p.nc
    TT = 512
    NT = T // TT
    x = d["x_fm"]
    xnz = d["xnz"]
    onescol = p.sb("p0_ones", [128, 1], F32)
    ones1 = p.sb("p0_ones1", [1, 128], F32)
    p.op("sync", lambda e: e.dma_start(out=onescol[:], in_=d["c_onescol"]),
         writes=("onescol",), dma=True)
    p.op("sync", lambda e: e.dma_start(out=ones1[:], in_=d["c_ones1"]),
         writes=("ones1",), dma=True)
    zt = p.sb("p0_zero", [128, 64], BF16)
    p.op("vector", lambda e: e.memset(zt[:], 0.0), writes=("zt",))
    p.op("sync", lambda e: e.dma_start(
        out=xnz[:, 0:8].rearrange("(c p) t -> p c t", p=128),
        in_=zt[:].rearrange("p (c t) -> p c t", t=8)),
        reads=("zt",), writes=("hbm.xnz.z",), dma=True)

    xt = [p.sb(f"p0_x{i}", [128, 8 * TT], F32) for i in range(2)]
    sqt = [p.sb(f"p0_sq{i}", [128, 8 * TT], F32) for i in range(2)]
    xno = [p.sb(f"p0_o{i}", [128, 8 * TT], BF16) for i in range(2)]
    tmp = [p.sb(f"p0_t{i}", [128, TT], F32) for i in range(2)]
    rows = p.sb("p0_rows", [1, 6 * TT], F32)
    psA = p.ps("p0_psA", [1, TT])
    psB = p.ps("p0_psB", [1, TT])
    psBC = p.ps("p0_psBC", [128, 2 * TT])

    for i in range(NT):
        b = i % 2
        t0 = TT * i
        kx = f"p0x{b}"
        p.op("sync", lambda e, t0=t0, b=b: e.dma_start(
            out=xt[b][:].rearrange("p (c t) -> p c t", t=TT),
            in_=_xtile_hbm(x, 0, 8, t0, TT)),
            reads=(), writes=(kx,), dma=True, tag=f"p0.load{i}")
        p.op("scalar", lambda e, b=b: e.activation(
            sqt[b][:], xt[b][:], AF.Square),
            reads=(kx,), writes=(f"p0sq{b}",))
        for c in range(8):
            p.op("tensor", lambda e, b=b, c=c: e.matmul(
                psA[:, :], onescol[0:128, :], xt[b][:, TT * c:TT * (c + 1)],
                start=(c == 0), stop=(c == 7)),
                reads=(kx, "onescol"), writes=("p0psA",))
        for c in range(8):
            p.op("tensor", lambda e, b=b, c=c: e.matmul(
                psB[:, :], onescol[0:128, :], sqt[b][:, TT * c:TT * (c + 1)],
                start=(c == 0), stop=(c == 7)),
                reads=(f"p0sq{b}", "onescol"), writes=("p0psB",))
        m_ = rows[:, 0 * TT:1 * TT]
        msq = rows[:, 1 * TT:2 * TT]
        v_ = rows[:, 2 * TT:3 * TT]
        a_ = rows[:, 3 * TT:4 * TT]
        b_ = rows[:, 4 * TT:5 * TT]
        p.op("vector", lambda e: e.tensor_scalar(m_, psA[:, :], 1.0 / D, None, ALU.mult),
             reads=("p0psA",), writes=("p0m",))
        p.op("vector", lambda e: e.tensor_tensor(msq, m_, m_, ALU.mult),
             reads=("p0m",), writes=("p0msq",))
        p.op("vector", lambda e: e.scalar_tensor_tensor(
            v_, psB[:, :], 1.0 / D, msq, ALU.mult, ALU.subtract),
            reads=("p0psB", "p0msq"), writes=("p0v",))
        p.op("vector", lambda e: e.tensor_scalar(v_, v_, 1e-5, None, ALU.add),
             reads=("p0v",), writes=("p0v",))
        p.op("scalar", lambda e: e.activation(v_, v_, AF.Ln),
             reads=("p0v",), writes=("p0v",))
        p.op("scalar", lambda e: e.activation(a_, v_, AF.Exp, scale=-0.5),
             reads=("p0v",), writes=("p0a",))
        p.op("vector", lambda e: e.scalar_tensor_tensor(
            b_, m_, -1.0, a_, ALU.mult, ALU.mult),
            reads=("p0m", "p0a"), writes=("p0b",))
        p.op("tensor", lambda e: e.matmul(
            psBC[:, 0:TT], ones1[0:1, :], a_, start=True, stop=True),
            reads=("p0a", "ones1"), writes=("p0bca",))
        p.op("tensor", lambda e: e.matmul(
            psBC[:, TT:2 * TT], ones1[0:1, :], b_, start=True, stop=True),
            reads=("p0b", "ones1"), writes=("p0bcb",))
        for c in range(8):
            tb = c % 2
            p.op("vector", lambda e, b=b, c=c, tb=tb: e.tensor_tensor(
                tmp[tb][:], xt[b][:, TT * c:TT * (c + 1)], psBC[:, 0:TT], ALU.mult),
                reads=(kx, "p0bca"), writes=(f"p0tmp{tb}",))
            p.op("vector", lambda e, b=b, c=c, tb=tb: e.tensor_tensor(
                xno[b][:, TT * c:TT * (c + 1)], tmp[tb][:], psBC[:, TT:2 * TT], ALU.add),
                reads=(f"p0tmp{tb}", "p0bcb"), writes=(f"p0o{b}",))
        p.op("sync", lambda e, t0=t0, b=b: e.dma_start(
            out=_xtile_hbm(xnz, 0, 8, t0, TT, pad=8),
            in_=xno[b][:].rearrange("p (c t) -> p c t", t=TT)),
            reads=(f"p0o{b}",), writes=(f"hbm.xnz.{i}",), dma=True,
            tag=f"p0.store{i}")


def emit_p1(p, d, stages="abcd"):
    nc = p.nc
    TT = 512
    NT = T // TT
    xnz = d["xnz"]
    s0 = d["s0"]
    decperm = d["t_decperm"]
    vbh = d["t_vb"]
    otok = d["t_otok"]

    ones1 = p.sb("p1_ones1", [1, 128], F32)
    p.op("sync", lambda e: e.dma_start(out=ones1[:], in_=d["c_ones1"]),
         writes=("ones1",), dma=True)
    id16f = p.sb("p1_id16", [16, 16], F32)
    p.op("sync", lambda e: e.dma_start(out=id16f[:], in_=d["c_id128f"][0:16, 0:16]),
         writes=("id16f",), dma=True)
    id128 = p.sb("p1_id128", [128, 128], BF16)
    p.op("sync", lambda e: e.dma_start(out=id128[:], in_=d["c_id128"]),
         writes=("id128",), dma=True)
    sel16 = p.sb("p1_sel16", [16, 1024], F32)
    p.op("sync", lambda e: e.dma_start(out=sel16[:], in_=d["c_sel16"]),
         writes=("sel16",), dma=True)
    mask01 = p.sb("p1_mask01", [64, 1024], BF16)
    p.op("sync", lambda e: e.dma_start(out=mask01[:], in_=d["c_mask01"]),
         writes=("mask01",), dma=True)
    dwT = p.sb("p1_dwT", [64, 1024], F32)
    gmT = p.sb("p1_gmT", [64, 1024], F32)
    gaT = p.sb("p1_gaT", [64, 1024], F32)
    decT = p.sb("p1_decT", [64, 1024], F32)
    elrow = p.sb("p1_elrow", [1, 1024], F32)
    beta_s = p.sb("p1_beta", [16, T], F32)

    sc = contextlib.ExitStack()
    wg_sb = p.sb_in(sc, "p1_wg", [128, 8 * 16], BF16)
    wb_sb = p.sb_in(sc, "p1_wb", [128, 8 * 16], BF16)
    wa_sb = p.sb_in(sc, "p1_wa", [128, 8 * 16], BF16)
    wgf = p.sb_in(sc, "p1_wgf", [128, 3 * 8 * 16], F32)
    alog = p.sb_in(sc, "p1_alog", [16, 1], F32)
    dtb = p.sb_in(sc, "p1_dtb", [16, 1], F32)
    for j, (nm, sb_, srcn) in enumerate((("wg", wg_sb, "WgT"), ("wb", wb_sb, "WbT"),
                                         ("wa", wa_sb, "WaT"))):
        p.op("sync", lambda e, j=j, srcn=srcn: e.dma_start(
            out=wgf[:, 128 * j:128 * (j + 1)].rearrange("p (c i) -> p c i", i=16),
            in_=d[srcn].rearrange("(c p) i -> p c i", p=128)),
            writes=(nm + "f",), dma=True)
        p.op("vector", lambda e, j=j, sb_=sb_: e.tensor_copy(
            sb_[:], wgf[:, 128 * j:128 * (j + 1)]),
            reads=(nm + "f",), writes=(nm,))
    p.op("sync", lambda e: e.dma_start(out=alog[:], in_=d["A_log"]),
         writes=("alog",), dma=True)
    p.op("sync", lambda e: e.dma_start(out=dtb[:], in_=d["dt_bias"]),
         writes=("dtb",), dma=True)
    negea = p.sb_in(sc, "p1_negea", [16, 1], F32)
    p.op("scalar", lambda e: e.activation(negea[:], alog[:], AF.Exp),
         reads=("alog",), writes=("negea",))

    xt = [p.sb_in(sc, f"p1_x{i}", [128, 8 * TT], BF16) for i in range(2)]
    gate_s = p.sb_in(sc, "p1_gate", [16, T], F32)
    al_s = p.sb_in(sc, "p1_al", [16, T], F32)
    sp_s = p.sb_in(sc, "p1_sp", [16, T], F32)
    deca = p.sb_in(sc, "p1_deca", [16, T], F32)
    decb = p.sb_in(sc, "p1_decb", [16, T], F32)
    dw_s = p.sb_in(sc, "p1_dw", [16, T], F32)
    ps16 = p.ps_in(sc, "p1_ps16", [16, TT])
    ps16b = p.ps_in(sc, "p1_ps16b", [16, TT])
    ps16c = p.ps_in(sc, "p1_ps16c", [16, TT])
    pstr = p.ps_in(sc, "p1_pstr", [64, 16])

    for i in range(NT):
        b = i % 2
        kx = f"p1x{b}"
        p.op("sync", lambda e, i=i, b=b: e.dma_start(
            out=xt[b][:].rearrange("p (c t) -> p c t", t=TT),
            in_=_xtile_hbm(xnz, 0, 8, TT * i, TT, pad=8)),
            writes=(kx,), dma=True, tag=f"p1.ldx{i}")
        c0 = TT * i
        for nm, wsb, ps_, kk in (("wg", wg_sb, ps16, "psg"),
                                 ("wb", wb_sb, ps16b, "psb"),
                                 ("wa", wa_sb, ps16c, "psa")):
            for c in range(8):
                p.op("tensor", lambda e, wsb=wsb, ps_=ps_, b=b, c=c: e.matmul(
                    ps_[:, :], wsb[:, 16 * c:16 * (c + 1)],
                    xt[b][:, TT * c:TT * (c + 1)],
                    start=(c == 0), stop=(c == 7)),
                    reads=(kx, nm), writes=(kk,))
        p.op("scalar", lambda e, c0=c0: e.activation(
            gate_s[:, c0:c0 + TT], ps16[:, :], AF.Sigmoid),
            reads=("psg",), writes=("gate",))
        p.op("scalar", lambda e, c0=c0: e.activation(
            beta_s[:, c0:c0 + TT], ps16b[:, :], AF.Sigmoid),
            reads=("psb",), writes=("beta",))
        p.op("vector", lambda e, c0=c0: e.tensor_scalar(
            al_s[:, c0:c0 + TT], ps16c[:, :], dtb[:], None, ALU.add),
            reads=("psa", "dtb"), writes=("al",))
    p.op("scalar", lambda e: e.activation(sp_s[:], al_s[:], AF.Abs),
         reads=("al",), writes=("sp",))
    p.op("scalar", lambda e: e.activation(sp_s[:], sp_s[:], AF.Exp, scale=-1.0),
         reads=("sp",), writes=("sp",))
    p.op("vector", lambda e: e.tensor_scalar(sp_s[:], sp_s[:], 1.0, None, ALU.add),
         reads=("sp",), writes=("sp",))
    p.op("scalar", lambda e: e.activation(sp_s[:], sp_s[:], AF.Ln),
         reads=("sp",), writes=("sp",))
    p.op("vector", lambda e: e.tensor_scalar(deca[:], al_s[:], 0.0, None, ALU.max),
         reads=("al",), writes=("deca",))
    p.op("vector", lambda e: e.tensor_tensor(deca[:], deca[:], sp_s[:], ALU.add),
         reads=("deca", "sp"), writes=("deca",))
    p.op("vector", lambda e: e.tensor_scalar(
        deca[:], deca[:], negea[:], -1.0, ALU.mult, ALU.mult),
        reads=("deca", "negea"), writes=("deca",))
    cur, oth = deca, decb
    kc, ko = "deca", "decb"
    for s in (1, 2, 4, 8, 16, 32):
        c3 = cur[:].rearrange("p (n c) -> p n c", c=64)
        o3 = oth[:].rearrange("p (n c) -> p n c", c=64)
        p.op("vector", lambda e, c3=c3, o3=o3, s=s: e.tensor_copy(
            o3[:, :, 0:s], c3[:, :, 0:s]), reads=(kc,), writes=(ko,))
        p.op("vector", lambda e, c3=c3, o3=o3, s=s: e.tensor_tensor(
            o3[:, :, s:64], c3[:, :, s:64], c3[:, :, 0:64 - s], ALU.add),
            reads=(kc,), writes=(ko,))
        cur, oth, kc, ko = oth, cur, ko, kc
    dec_s = cur
    kdec = kc
    d3 = dec_s[:].rearrange("p (n c) -> p n c", c=64)
    dlast = d3[:, :, 63:64].broadcast_to((16, 64, 64))
    p.op("vector", lambda e: e.tensor_tensor(
        dw_s[:].rearrange("p (n c) -> p n c", c=64), dlast, d3, ALU.subtract),
        reads=(kdec,), writes=("dwl",))
    p.op("scalar", lambda e: e.activation(dw_s[:], dw_s[:], AF.Exp),
         reads=("dwl",), writes=("dwl",))
    dexp_s = sp_s
    p.op("scalar", lambda e: e.activation(dexp_s[:], dec_s[:], AF.Exp),
         reads=(kdec,), writes=("dexp",))
    gm_s = al_s
    p.op("vector", lambda e: e.tensor_tensor(gm_s[:], dexp_s[:], gate_s[:], ALU.mult),
         reads=("dexp", "gate"), writes=("gm",))
    p.op("sync", lambda e: e.dma_start(
        out=decperm.rearrange("n (h c) -> h n c", h=16),
        in_=dec_s[:].rearrange("p (n c) -> p n c", c=64)),
        reads=(kdec,), writes=("hbm.decperm",), dma=True)
    for qnm, qsrc, ksrc, dst, kdst in (
            ("dec", dec_s, kdec, decT, "decT"), ("dw", dw_s, "dwl", dwT, "dwT"),
            ("gm", gm_s, "gm", gmT, "gmT"), ("ga", gate_s, "gate", gaT, "gaT")):
        for n in range(64):
            p.op("tensor", lambda e, qsrc=qsrc, n=n: e.transpose(
                pstr[:, :], qsrc[:, 64 * n:64 * (n + 1)], id16f[:]),
                reads=(ksrc, "id16f"), writes=("pstr",))
            p.op("vector", lambda e, dst=dst, n=n: e.tensor_copy(
                dst[:, 16 * n:16 * (n + 1)], pstr[:, :]),
                reads=("pstr",), writes=(kdst,))
    p.op("sync", lambda e: e.dma_start(out=elrow[:], in_=decT[63:64, :]),
         reads=("decT",), writes=("elrow",), dma=True)
    p.op("scalar", lambda e: e.activation(elrow[:], elrow[:], AF.Exp),
         reads=("elrow",), writes=("elrow",))
    p.barrier()
    sc.close()

    if "b" not in stages:
        return
    sc = contextlib.ExitStack()
    ww_sb = p.sb_in(sc, "p1_ww", [128, 8 * 1024], BF16)
    ww_f = p.sb_in(sc, "p1_wwf", [128, 8 * 1024], F32)
    p.op("sync", lambda e: e.dma_start(
        out=ww_f[:].rearrange("p (c i) -> p c i", i=1024),
        in_=d["WwT"].rearrange("(c p) i -> p c i", p=128)),
        writes=("wwf",), dma=True)
    p.op("vector", lambda e: e.tensor_copy(ww_sb[:], ww_f[:]),
         reads=("wwf",), writes=("ww",))
    xt2 = [p.sb_in(sc, f"p1_x2{i}", [128, 8 * 128], BF16) for i in range(2)]
    bcb = [p.sb_in(sc, f"p1_bcb{i}", [128, 1024], F32) for i in range(2)]
    vbt = [p.sb_in(sc, f"p1_vbt{i}", [128, 1024], BF16) for i in range(2)]
    dly = p.sb_in(sc, "p1_dly", [128, 512], F32)
    vraw = [p.sb_in(sc, f"p1_vraw{i}", [128, 1024], F32) for i in range(2)]
    psv = [p.ps_in(sc, f"p1_psv{j}", [128, 1024]) for j in range(2)]
    psbc = [p.ps_in(sc, f"p1_psbc{j}", [128, 1024]) for j in range(2)]
    for i in range(32):
        b = i % 2
        kx = f"p1x2{b}"
        p.op("sync", lambda e, i=i, b=b: e.dma_start(
            out=xt2[b][:].rearrange("p (c t) -> p c t", t=128),
            in_=_xtile_hbm(xnz, 0, 8, 128 * i, 128, pad=8)),
            writes=(kx,), dma=True)
        for half in range(2):
            kb = f"psbc{b}{half}"
            p.op("tensor", lambda e, i=i, b=b, half=half: e.matmul(
                psbc[b][:, 512 * half:512 * (half + 1)],
                beta_s[:, 128 * i:128 * (i + 1)],
                sel16[:, 512 * half:512 * (half + 1)],
                start=True, stop=True),
                reads=("beta", "sel16"), writes=(kb,))
            p.op("vector", lambda e, b=b, half=half: e.tensor_copy(
                bcb[b][:, 512 * half:512 * (half + 1)],
                psbc[b][:, 512 * half:512 * (half + 1)]),
                reads=(kb,), writes=(f"bcb{b}",))
        for half in range(2):
            for c in range(8):
                p.op("tensor", lambda e, b=b, c=c, half=half: e.matmul(
                    psv[b][:, 512 * half:512 * (half + 1)],
                    xt2[b][:, 128 * c:128 * (c + 1)],
                    ww_sb[:, 1024 * c + 512 * half:1024 * c + 512 * (half + 1)],
                    start=(c == 0), stop=(c == 7)),
                    reads=(kx, "ww", f"bcb{b}"), writes=(f"psv{b}{half}",))
        if P1B_DBG:
            for half in range(2):
                p.op("vector", lambda e, b=b, half=half: e.tensor_copy(
                    vraw[b][:, 512 * half:512 * (half + 1)],
                    psv[b][:, 512 * half:512 * (half + 1)]),
                    reads=(f"psv{b}0", f"psv{b}1"), writes=(f"vraw{b}",))
            p.op("sync", lambda e, i=i, b=b: e.dma_start(
                out=d["t_vraw"][128 * i:128 * (i + 1), :], in_=vraw[b][:]),
                reads=(f"vraw{b}",), writes=("hbm.vraw",), dma=True)
            p.op("sync", lambda e, i=i, b=b: e.dma_start(
                out=d["t_bcb"][128 * i:128 * (i + 1), :], in_=bcb[b][:]),
                reads=(f"bcb{b}",), writes=("hbm.bcb",), dma=True)
        if P1B_DELAY:
            p.op("vector", lambda e: e.tensor_scalar(
                dly[:], dly[:], 1.0, None, ALU.mult),
                reads=("psvdly",), writes=("psvdly",))
            p.op("vector", lambda e: e.tensor_scalar(
                dly[:], dly[:], 1.0, None, ALU.mult),
                reads=("psvdly",), writes=("psvdly",))
        for half in range(2):
            p.op("vector", lambda e, b=b, half=half: e.tensor_tensor(
                vbt[b][:, 512 * half:512 * (half + 1)],
                psv[b][:, 512 * half:512 * (half + 1)],
                bcb[b][:, 512 * half:512 * (half + 1)], ALU.mult),
                reads=(f"psv{b}0", f"psv{b}1", f"bcb{b}"), writes=(f"vbt{b}",))
        p.op("sync", lambda e, i=i, b=b: e.dma_start(
            out=vbh[128 * i:128 * (i + 1), :], in_=vbt[b][:]),
            reads=(f"vbt{b}",), writes=("hbm.vb",), dma=True)
        if P1B_SERIAL:
            p.barrier()
    p.barrier()
    sc.close()

    if "c" not in stages:
        return
    sc = contextlib.ExitStack()
    xts = [p.sb_in(sc, f"p1_xs{i}", [64, 16 * 66], BF16) for i in range(2)]
    vbs = [p.sb_in(sc, f"p1_vbs{i}", [64, 1024], BF16) for i in range(2)]
    dfl = [p.sb_in(sc, f"p1_dfl{i}", [1, 1024], F32) for i in range(2)]
    wkts = p.sb_in(sc, "p1_wkts", [64, 1024], BF16)
    dts = p.sb_in(sc, "p1_dts", [64, 1024], F32)
    lts = p.sb_in(sc, "p1_lts", [64, 1024], F32)
    rawm = p.sb_in(sc, "p1_rawm", [64, 1024], BF16)
    vt2 = p.sb_in(sc, "p1_vt2", [64, 1024], BF16)
    t1s = p.sb_in(sc, "p1_t1", [64, 1024], F32)
    t2s = p.sb_in(sc, "p1_t2", [64, 1024], F32)
    obf = p.sb_in(sc, "p1_obf", [64, 1024], BF16)
    Sf = p.sb_in(sc, "p1_Sf", [64, 1024], F32)
    Sb = p.sb_in(sc, "p1_Sb", [64, 1024], BF16)
    elbc = p.sb_in(sc, "p1_elbc", [64, 1024], F32)
    psA = p.ps_in(sc, "p1_psA", [64, 1024])
    psB = p.ps_in(sc, "p1_psB", [64, 1024])
    psC = p.ps_in(sc, "p1_psC", [64, 1024])
    pswk = p.ps_in(sc, "p1_pswk", [64, 1024], BF16)
    p.op("vector", lambda e: e.memset(Sf[:], 0.0), writes=("Sf",))
    p.op("vector", lambda e: e.memset(Sb[:], 0.0), writes=("Sb",))
    for half in range(2):
        p.op("tensor", lambda e, half=half: e.matmul(
            psA[:, 512 * half:512 * (half + 1)], ones1[0:1, 0:64],
            elrow[0:1, 512 * half:512 * (half + 1)], start=True, stop=True),
            reads=("elrow", "ones1"), writes=("psA",))
        p.op("vector", lambda e, half=half: e.tensor_copy(
            elbc[:, 512 * half:512 * (half + 1)],
            psA[:, 512 * half:512 * (half + 1)]),
            reads=("psA",), writes=("elbc",))

    xnz_hd = xnz.rearrange("(h dd) t -> dd h t", dd=64)

    for n in range(P1C_LIMIT):
        b = n % 2
        kx = f"p1xs{b}"
        p.op("sync", lambda e, n=n, b=b: e.dma_start(
            out=xts[b][:].rearrange("p (h t) -> p h t", t=66)[:, :, 0:65],
            in_=xnz_hd[:, :, 8 + 64 * n - 1: 8 + 64 * n + 64]),
            writes=(kx,), dma=True, tag=f"p1c.x{n}")
        p.op("sync", lambda e, n=n, b=b: e.dma_start(
            out=vbs[b][:], in_=vbh[64 * n:64 * (n + 1), :]),
            reads=("hbm.vb",), writes=(f"vbs{b}",), dma=True)
        p.op("sync", lambda e, n=n, b=b: e.dma_start(
            out=dfl[b][:], in_=decperm[n:n + 1, :]),
            reads=("hbm.decperm",), writes=(f"dfl{b}",), dma=True)

        def rk(h, b=b):
            return xts[b][0:64, 66 * h + 1:66 * h + 65]

        def wkf(h, b=b):
            return xts[b][0:64, 66 * h + 0:66 * h + 64]

        for h in range(16):
            p.op("tensor", lambda e, h=h, b=b: e.transpose(
                pswk[:, 64 * h:64 * (h + 1)], wkf(h, b), id128[0:64, 0:64]),
                reads=(kx, "id128"), writes=("pswk",))
        p.op("vector", lambda e: e.tensor_copy(wkts[:], pswk[:]),
             reads=("pswk",), writes=("wkts",))
        for h in range(16):
            p.op("tensor", lambda e, h=h, b=b: e.matmul(
                psB[:, 64 * h:64 * (h + 1)], wkf(h, b), rk(h, b),
                start=True, stop=True), reads=(kx,), writes=("psB",))
        for half in range(2):
            p.op("tensor", lambda e, b=b, half=half: e.matmul(
                psA[:, 512 * half:512 * (half + 1)], ones1[0:1, 0:64],
                dfl[b][0:1, 512 * half:512 * (half + 1)],
                start=True, stop=True),
                reads=(f"dfl{b}", "ones1"), writes=("psA",))
        dtv = decT[:, 16 * n:16 * (n + 1)]
        p.op("vector", lambda e, dtv=dtv: e.tensor_tensor(
            dts[:].rearrange("p (h c) -> p h c", c=64),
            psA[:, :].rearrange("p (h c) -> p h c", c=64),
            dtv.unsqueeze(2).broadcast_to((64, 16, 64)), ALU.subtract),
            reads=("psA", "decT"), writes=("dts",))
        p.op("vector", lambda e: e.tensor_scalar(
            dts[:], dts[:], 0.0, None, ALU.min), reads=("dts",), writes=("dts",))
        p.op("scalar", lambda e: e.activation(lts[:], dts[:], AF.Exp),
             reads=("dts",), writes=("lts",))
        p.op("vector", lambda e: e.tensor_tensor(
            lts[:], lts[:], mask01[:], ALU.mult),
            reads=("lts", "mask01"), writes=("lts",))
        p.op("vector", lambda e: e.tensor_tensor(
            rawm[:], psB[:, :], lts[:], ALU.mult),
            reads=("psB", "lts"), writes=("rawm",))
        for h in range(16):
            p.op("tensor", lambda e, h=h, b=b: e.matmul(
                psA[:, 64 * h:64 * (h + 1)], rk(h, b),
                Sb[0:64, 64 * h:64 * (h + 1)],
                start=True, stop=True), reads=(kx, "Sb"), writes=("psA",))
        for h in range(16):
            p.op("tensor", lambda e, h=h, b=b: e.matmul(
                psC[:, 64 * h:64 * (h + 1)], rawm[:, 64 * h:64 * (h + 1)],
                vbs[b][:, 64 * h:64 * (h + 1)],
                start=True, stop=True), reads=("rawm", f"vbs{b}"), writes=("psC",))
        dwv = dwT[:, 16 * n:16 * (n + 1)]
        p.op("vector", lambda e, b=b, dwv=dwv: e.tensor_tensor(
            vt2[:].rearrange("p (h c) -> p h c", c=64),
            vbs[b][:].rearrange("p (h c) -> p h c", c=64),
            dwv.unsqueeze(2).broadcast_to((64, 16, 64)), ALU.mult),
            reads=(f"vbs{b}", "dwT"), writes=("vt2",))
        for h in range(16):
            p.op("tensor", lambda e, h=h: e.matmul(
                psB[:, 64 * h:64 * (h + 1)], wkts[:, 64 * h:64 * (h + 1)],
                vt2[:, 64 * h:64 * (h + 1)],
                start=True, stop=True), reads=("wkts", "vt2"), writes=("psB",))
        gmv = gmT[:, 16 * n:16 * (n + 1)]
        gav = gaT[:, 16 * n:16 * (n + 1)]
        p.op("vector", lambda e, gmv=gmv: e.tensor_tensor(
            t1s[:].rearrange("p (h c) -> p h c", c=64),
            psA[:, :].rearrange("p (h c) -> p h c", c=64),
            gmv.unsqueeze(2).broadcast_to((64, 16, 64)), ALU.mult),
            reads=("psA", "gmT"), writes=("t1s",))
        p.op("vector", lambda e, gav=gav: e.tensor_tensor(
            t2s[:].rearrange("p (h c) -> p h c", c=64),
            psC[:, :].rearrange("p (h c) -> p h c", c=64),
            gav.unsqueeze(2).broadcast_to((64, 16, 64)), ALU.mult),
            reads=("psC", "gaT"), writes=("t2s",))
        p.op("vector", lambda e: e.tensor_tensor(
            obf[:], t1s[:], t2s[:], ALU.add),
            reads=("t1s", "t2s"), writes=("obf",))
        p.op("sync", lambda e, n=n: e.dma_start(
            out=otok[64 * n:64 * (n + 1), :], in_=obf[:]),
            reads=("obf",), writes=("hbm.otok",), dma=True)
        elv = elbc[:, 16 * n:16 * (n + 1)]
        p.op("vector", lambda e, elv=elv: e.tensor_tensor(
            t1s[:].rearrange("p (h c) -> p h c", c=64),
            Sf[:].rearrange("p (h c) -> p h c", c=64),
            elv.unsqueeze(2).broadcast_to((64, 16, 64)), ALU.mult),
            reads=("Sf", "elbc"), writes=("t1s",))
        p.op("vector", lambda e: e.tensor_tensor(
            Sf[:], t1s[:], psB[:, :], ALU.add),
            reads=("t1s", "psB"), writes=("Sf",))
        p.op("vector", lambda e: e.tensor_copy(Sb[:], Sf[:]),
             reads=("Sf",), writes=("Sb",))
        if P1C_SDBG:
            p.op("sync", lambda e, n=n: e.dma_start(
                out=d["t_sdbg"][64 * n:64 * (n + 1), :], in_=Sf[:]),
                reads=("Sf",), writes=("hbm.sdbg",), dma=True)
    p.barrier()
    sc.close()

    if "d" not in stages:
        return
    sc = contextlib.ExitStack()
    wout_sb = p.sb_in(sc, "p1_wout", [128, 8 * 1024], BF16)
    wout_f = p.sb_in(sc, "p1_woutf", [128, 8 * 1024], F32)
    p.op("sync", lambda e: e.dma_start(
        out=wout_f[:].rearrange("p (c i) -> p c i", i=1024),
        in_=d["WoutT"].rearrange("(c p) i -> p c i", p=128)),
        writes=("woutf",), dma=True)
    p.op("vector", lambda e: e.tensor_copy(wout_sb[:], wout_f[:]),
         reads=("woutf",), writes=("wout",))
    ot = [p.sb_in(sc, f"p1_ot{i}", [128, 1024], BF16) for i in range(2)]
    ofm = [p.sb_in(sc, f"p1_ofm{i}", [128, 4 * 8 * 128], BF16) for i in range(2)]
    xt3 = [p.sb_in(sc, f"p1_x3{i}", [128, 8 * TT], BF16) for i in range(2)]
    s0t = [p.sb_in(sc, f"p1_s0{i}", [128, TT], F32) for i in range(2)]
    psot = p.ps_in(sc, "p1_psot", [128, 1024], BF16)
    psmm = [p.ps_in(sc, f"p1_psmm{i}", [128, TT]) for i in range(2)]
    for i in range(NT):
        bb = i % 2
        kx = f"p1x3{bb}"
        p.op("sync", lambda e, i=i, bb=bb: e.dma_start(
            out=xt3[bb][:].rearrange("p (c t) -> p c t", t=TT),
            in_=_xtile_hbm(xnz, 0, 8, TT * i, TT, pad=8)),
            writes=(kx,), dma=True)
        for s4 in range(4):
            tt = 4 * i + s4
            bo = s4 % 2
            ko = f"p1ot{bo}"
            p.op("sync", lambda e, tt=tt, bo=bo: e.dma_start(
                out=ot[bo][:], in_=otok[128 * tt:128 * (tt + 1), :]),
                reads=("hbm.otok",), writes=(ko,), dma=True)
            for c in range(8):
                p.op("tensor", lambda e, bo=bo, c=c: e.transpose(
                    psot[:, 128 * c:128 * (c + 1)],
                    ot[bo][:, 128 * c:128 * (c + 1)], id128[:]),
                    reads=(ko, "id128"), writes=("psot",))
            p.op("vector", lambda e, bb=bb, s4=s4: e.tensor_copy(
                ofm[bb][:].rearrange("p (c s t) -> p c s t", s=4, t=128)
                [:, :, s4, :],
                psot[:, :].rearrange("p (c t) -> p c t", t=128)),
                reads=("psot",), writes=(f"p1ofm{bb}",))
        for oc in range(8):
            pb = oc % 2
            for c in range(8):
                p.op("tensor", lambda e, bb=bb, oc=oc, c=c, pb=pb: e.matmul(
                    psmm[pb][:, :],
                    wout_sb[:, 1024 * c + 128 * oc:1024 * c + 128 * (oc + 1)],
                    ofm[bb][:, 512 * c:512 * (c + 1)],
                    start=(c == 0), stop=(c == 7)),
                    reads=(f"p1ofm{bb}", "wout"), writes=(f"psmm{pb}",))
            p.op("vector", lambda e, bb=bb, oc=oc, pb=pb: e.tensor_tensor(
                s0t[pb][:], psmm[pb][:, :],
                xt3[bb][:, TT * oc:TT * (oc + 1)], ALU.add),
                reads=(f"psmm{pb}", kx), writes=(f"s0t{pb}",))
            p.op("sync", lambda e, i=i, oc=oc, pb=pb: e.dma_start(
                out=s0[128 * oc:128 * (oc + 1), TT * i:TT * (i + 1)],
                in_=s0t[pb][:]),
                reads=(f"s0t{pb}",), writes=("hbm.s0",), dma=True)
    p.barrier()
    sc.close()


def phase_p1(stages="abcd", dbg=False):
    specs = [
        ("xnz", (D, T + 8), BF16, "ExternalInput"),
        ("WwT", (D, D), F32, "ExternalInput"),
        ("WgT", (D, 16), F32, "ExternalInput"),
        ("WbT", (D, 16), F32, "ExternalInput"),
        ("WaT", (D, 16), F32, "ExternalInput"),
        ("WoutT", (D, D), F32, "ExternalInput"),
        ("A_log", (16, 1), F32, "ExternalInput"),
        ("dt_bias", (16, 1), F32, "ExternalInput"),
        ("c_ones1", (1, 128), F32, "ExternalInput"),
        ("c_id128", (128, 128), BF16, "ExternalInput"),
        ("c_id128f", (128, 128), F32, "ExternalInput"),
        ("c_sel16", (16, 1024), F32, "ExternalInput"),
        ("c_mask01", (64, 1024), BF16, "ExternalInput"),
        ("t_decperm", (64, 1024), F32, "ExternalOutput" if dbg else "Internal"),
        ("t_vraw", (T, 1024), F32, "ExternalOutput" if dbg else "Internal"),
        ("t_sdbg", (T, 1024), F32, "ExternalOutput" if dbg else "Internal"),
        ("t_bcb", (T, 1024), F32, "ExternalOutput" if dbg else "Internal"),
        ("t_vb", (T, 1024), BF16, "ExternalOutput" if dbg else "Internal"),
        ("t_otok", (T, 1024), BF16, "ExternalOutput" if dbg else "Internal"),
        ("s0", (D, T), F32, "ExternalOutput"),
    ]
    ins = ["xnz", "WwT", "WgT", "WbT", "WaT", "WoutT", "A_log", "dt_bias",
           "c_ones1", "c_id128", "c_id128f", "c_sel16", "c_mask01"]
    outs = ["s0"] + (["t_decperm", "t_vb", "t_otok", "t_vraw", "t_bcb", "t_sdbg"] if dbg else [])
    return _mk_nc(lambda p, d: emit_p1(p, d, stages), specs), ins, outs




def emit_ffn(p, d, l, yin, yin_f32, yout, stages="ABCDEFG"):
    nc = p.nc
    TT = 512
    lnh = d["t_lnh"]
    hg = d["t_hg"]
    ymid = d["t_ymid"]

    ones1 = p.sb(f"f{l}_ones1", [1, 128], F32)
    p.op("sync", lambda e: e.dma_start(out=ones1[:], in_=d["c_ones1"]),
         writes=("ones1",), dma=True)
    onescol = p.sb(f"f{l}_onescol", [128, 1], F32)
    p.op("sync", lambda e: e.dma_start(out=onescol[:], in_=d["c_onescol"]),
         writes=("onescol",), dma=True)
    blockones = p.sb(f"f{l}_bo", [128, 2], F32)
    p.op("sync", lambda e: e.dma_start(out=blockones[:], in_=d["c_blockones"]),
         writes=("bo",), dma=True)
    sel2 = p.sb(f"f{l}_sel2", [2, 128], F32)
    p.op("sync", lambda e: e.dma_start(out=sel2[:], in_=d["c_sel2"]),
         writes=("sel2",), dma=True)
    id128 = p.sb(f"f{l}_id", [128, 128], BF16)
    p.op("sync", lambda e: e.dma_start(out=id128[:], in_=d["c_id128"]),
         writes=("id128",), dma=True)
    rot = p.sb(f"f{l}_rot", [128, 128], BF16)
    p.op("sync", lambda e: e.dma_start(out=rot[:], in_=d["c_rot"]),
         writes=("rot",), dma=True)
    qscol = p.sb(f"f{l}_qs", [128, 1], F32)
    p.op("sync", lambda e: e.dma_start(out=qscol[:], in_=d[f"qs{l}"]),
         writes=("qs",), dma=True)
    kscol = p.sb(f"f{l}_ks", [128, 1], F32)
    p.op("sync", lambda e: e.dma_start(out=kscol[:], in_=d[f"ks{l}"]),
         writes=("ks",), dma=True)

    def wload(es_, name, src_ap, kc, cols, parts=8):
        wsb = p.sb_in(es_, name, [128, parts * cols], BF16)
        sc0 = contextlib.ExitStack()
        tmp = p.sb_in(sc0, name + "_f", [128, 2 * cols], F32)
        for g in range(0, parts, 2):
            npair = min(2, parts - g)
            p.op("sync", lambda e, g=g, npair=npair: e.dma_start(
                out=tmp[:, 0:npair * cols].rearrange("p (c i) -> p c i", i=cols),
                in_=src_ap[g * 128:(g + npair) * 128, :]
                .rearrange("(c p) i -> p c i", p=128)),
                writes=(name + "_f",), dma=True)
            p.op("vector", lambda e, g=g, npair=npair: e.tensor_copy(
                wsb[:, g * cols:(g + npair) * cols], tmp[:, 0:npair * cols]),
                reads=(name + "_f",), writes=(name,))
        p.barrier()
        sc0.close()
        return wsb

    def load_y(dst, kdst, t0, tn, shifted, scratch=None, kscratch=None):
        off = 8 + t0
        if yin_f32:
            assert scratch is not None
            for half in range(2):
                sh = 1 if (shifted and half == 1) else 0
                p.op("sync", lambda e, half=half, sh=sh: e.dma_start(
                    out=scratch[:, half * 4 * tn:(half + 1) * 4 * tn]
                    .rearrange("p (c t) -> p c t", t=tn),
                    in_=_xtile_hbm(yin, 4 * half, 4, t0 - sh, tn, pad=8)),
                    writes=(kscratch,), dma=True)
            p.op("vector", lambda e: e.tensor_copy(dst[:], scratch[:]),
                 reads=(kscratch,), writes=(kdst,))
        else:
            for half in range(2):
                sh = 1 if (shifted and half == 1) else 0
                p.op("sync", lambda e, half=half, sh=sh: e.dma_start(
                    out=dst[:, half * 4 * tn:(half + 1) * 4 * tn]
                    .rearrange("p (c t) -> p c t", t=tn),
                    in_=_xtile_hbm(yin, 4 * half, 4, t0 - sh, tn, pad=8)),
                    writes=(kdst,), dma=True)

    scAD = contextlib.ExitStack()
    qp = p.sb_in(scAD, f"f{l}_qp", [128, 4 * T], BF16)
    kp = p.sb_in(scAD, f"f{l}_kp", [128, 4 * (512 + T)], BF16)
    vtok = p.sb_in(scAD, f"f{l}_vtok", [128, 32 * 512], BF16)
    p.op("vector", lambda e: e.memset(kp[:], 0.0), writes=("kp",))

    if "A" in stages:
        sc = contextlib.ExitStack()
        qkvw = wload(sc, f"f{l}_qkv", d[f"qkv{l}"], 8, 1536)
        tabs = {}
        for nm in ("csq", "ssq", "csk", "ssk"):
            tabs[nm] = p.sb_in(sc, f"f{l}_{nm}", [128, TT], BF16)
        xta = [p.sb_in(sc, f"f{l}a_x{i}", [128, 8 * TT], BF16) for i in range(2)]
        xsca = ([p.sb_in(sc, f"f{l}a_xs0", [128, 8 * TT], F32)] * 2
                if yin_f32 else [None, None])
        vtmp4 = p.sb_in(sc, f"f{l}_vt4", [128, 4 * TT], BF16)
        qkw = [p.sb_in(sc, f"f{l}a_qk{i}", [128, 8 * TT], F32) for i in range(2)]
        sqta = p.sb_in(sc, f"f{l}a_sq", [128, TT], F32)
        srt = p.sb_in(sc, f"f{l}a_srt", [2, TT], F32)
        qn = p.sb_in(sc, f"f{l}a_qn", [128, TT], BF16)
        t1a = p.sb_in(sc, f"f{l}a_t1", [128, TT], F32)
        t2a = p.sb_in(sc, f"f{l}a_t2", [128, TT], F32)
        qf = p.sb_in(sc, f"f{l}a_qf", [128, TT], BF16)
        psq = [p.ps_in(sc, f"f{l}a_ps{i}", [128, TT]) for i in range(2)]
        psvt = p.ps_in(sc, f"f{l}_psvt", [128, 512], BF16)
        ps2 = p.ps_in(sc, f"f{l}a_ps2", [2, TT])
        psbc = p.ps_in(sc, f"f{l}a_psbc", [128, TT])
        psr = p.ps_in(sc, f"f{l}a_psr", [128, TT])
        for i in range(8):
            b = i % 2
            kx = f"f{l}a_x{b}"
            t0 = TT * i
            load_y(xta[b], kx, t0, TT, shifted=True, scratch=xsca[b],
                   kscratch=f"f{l}a_xs0")
            for nm in ("csq", "ssq", "csk", "ssk"):
                p.op("sync", lambda e, nm=nm, t0=t0: e.dma_start(
                    out=tabs[nm][:], in_=d["c_" + nm][:, t0:t0 + TT]),
                    writes=(nm,), dma=True)
            for oc in range(8, 12):
                pb = oc % 2
                for c in range(8):
                    p.op("tensor", lambda e, b=b, oc=oc, c=c, pb=pb: e.matmul(
                        psq[pb][:, :], qkvw[:, 1536 * c + 128 * oc:
                                            1536 * c + 128 * (oc + 1)],
                        xta[b][:, TT * c:TT * (c + 1)],
                        start=(c == 0), stop=(c == 7)),
                        reads=(kx, f"f{l}_qkv"), writes=(f"psq{pb}",))
                p.op("vector", lambda e, oc=oc, pb=pb: e.tensor_copy(
                    vtmp4[:, (oc - 8) * TT:(oc - 7) * TT],
                    psq[pb][:, :]), reads=(f"psq{pb}",), writes=("vtmp4",))
            for s in range(4):
                for vc in range(4):
                    p.op("tensor", lambda e, s=s, vc=vc: e.transpose(
                        psvt[:, 128 * vc:128 * (vc + 1)],
                        vtmp4[:, vc * TT + 128 * s:vc * TT + 128 * (s + 1)],
                        id128[:]),
                        reads=("vtmp4", "id128"), writes=("psvt",))
                p.op("vector", lambda e, i=i, s=s: e.tensor_copy(
                    vtok[:, 512 * (4 * i + s):512 * (4 * i + s + 1)],
                    psvt[:, :]), reads=("psvt",), writes=("vtok",))
            for oc in range(8):
                pb = oc % 2
                for c in range(8):
                    p.op("tensor", lambda e, b=b, oc=oc, c=c, pb=pb: e.matmul(
                        psq[pb][:, :], qkvw[:, 1536 * c + 128 * oc:
                                            1536 * c + 128 * (oc + 1)],
                        xta[b][:, TT * c:TT * (c + 1)],
                        start=(c == 0), stop=(c == 7)),
                        reads=(kx, f"f{l}_qkv"), writes=(f"psq{pb}",))
                p.op("vector", lambda e, b=b, oc=oc, pb=pb: e.tensor_copy(
                    qkw[b][:, TT * oc:TT * (oc + 1)], psq[pb][:, :]),
                    reads=(f"psq{pb}",), writes=(f"f{l}a_qk{b}",))
            for oc in range(8):
                isq = oc < 4
                scol = qscol if isq else kscol
                cs = tabs["csq"] if isq else tabs["csk"]
                ss = tabs["ssq"] if isq else tabs["ssk"]
                kqk = f"f{l}a_qk{b}"
                qsl = qkw[b][:, TT * oc:TT * (oc + 1)]
                p.op("scalar", lambda e, qsl=qsl: e.activation(
                    sqta[:], qsl, AF.Square), reads=(kqk,), writes=("sqta",))
                p.op("tensor", lambda e: e.matmul(
                    ps2[:, :], blockones[:], sqta[:], start=True, stop=True),
                    reads=("sqta", "bo"), writes=("ps2",))
                p.op("scalar", lambda e: e.activation(srt[:], ps2[:, :], AF.Sqrt),
                     reads=("ps2",), writes=("srt",))
                p.op("vector", lambda e: e.tensor_scalar(
                    srt[:], srt[:], 1e-12, None, ALU.max),
                    reads=("srt",), writes=("srt",))
                p.op("vector", lambda e: e.reciprocal(srt[:], srt[:]),
                     reads=("srt",), writes=("srt",))
                p.op("tensor", lambda e: e.matmul(
                    psbc[:, :], sel2[:], srt[:], start=True, stop=True),
                    reads=("srt", "sel2"), writes=("psbc",))
                p.op("vector", lambda e, qsl=qsl, scol=scol: e.scalar_tensor_tensor(
                    qn[:], qsl, scol[:], psbc[:, :], ALU.mult, ALU.mult),
                    reads=(kqk, "psbc", "qs" if isq else "ks"), writes=("qn",))
                p.op("tensor", lambda e: e.matmul(
                    psr[:, :], rot[:], qn[:], start=True, stop=True),
                    reads=("qn", "rot"), writes=("psr",))
                p.op("vector", lambda e, ss=ss: e.tensor_tensor(
                    t1a[:], psr[:, :], ss[:], ALU.mult),
                    reads=("psr", "ssq", "ssk"), writes=("t1a",))
                p.op("vector", lambda e, cs=cs: e.tensor_tensor(
                    t2a[:], qn[:], cs[:], ALU.mult),
                    reads=("qn", "csq", "csk"), writes=("t2a",))
                p.op("vector", lambda e: e.tensor_tensor(
                    qf[:], t1a[:], t2a[:], ALU.add),
                    reads=("t1a", "t2a"), writes=("qf",))
                if isq:
                    p.op("vector", lambda e, oc=oc, t0=t0: e.tensor_copy(
                        qp[:, oc * T + t0:oc * T + t0 + TT], qf[:]),
                        reads=("qf",), writes=("qp",))
                else:
                    p.op("vector", lambda e, oc=oc, t0=t0: e.tensor_copy(
                        kp[:, (oc - 4) * (512 + T) + 512 + t0:
                              (oc - 4) * (512 + T) + 512 + t0 + TT], qf[:]),
                        reads=("qf",), writes=("kp",))
        p.barrier()
        sc.close()

    scCD = contextlib.ExitStack()
    ofm = p.sb_in(scCD, f"f{l}_ofm", [128, 4 * T], BF16)

    if "C" in stages:
        sc = contextlib.ExitStack()
        masks = p.sb_in(sc, f"f{l}_masks", [128, 8 * 1024], BF16)
        p.op("sync", lambda e: e.dma_start(
            out=masks[:].rearrange("p (m j) -> p m j", j=1024),
            in_=d["c_attnmask"].rearrange("(m p) j -> p m j", p=128)),
            writes=("masks",), dma=True)
        NU = 3
        ssb = [p.sb_in(sc, f"f{l}c_s{i}", [128, 1024], F32) for i in range(NU)]
        mx = [p.sb_in(sc, f"f{l}c_mx{i}", [128, 1], F32) for i in range(NU)]
        sm = [p.sb_in(sc, f"f{l}c_sm{i}", [128, 1], F32) for i in range(NU)]
        abf = [p.sb_in(sc, f"f{l}c_a{i}", [128, 1024], BF16) for i in range(NU)]
        atT = [p.sb_in(sc, f"f{l}c_at{i}", [128, 1024], BF16) for i in range(NU)]
        psS = [p.ps_in(sc, f"f{l}c_psS{i}", [128, 1024]) for i in range(2)]
        psT = [p.ps_in(sc, f"f{l}c_psT{i}", [128, 1024], BF16) for i in range(2)]
        psO = p.ps_in(sc, f"f{l}c_psO", [128, 3 * 128])
        otm = p.sb_in(sc, f"f{l}c_otm", [128, 32 * 64], BF16)
        psB2 = p.ps_in(sc, f"f{l}c_psB2", [128, 1024], BF16)
        items = [(h, qt) for h in range(8) for qt in range(32)]

        def c_scores(idx):
            h, qt = items[idx]
            hb = 64 * (h % 2)
            qrow = (h // 2) * T
            krow = (h // 2) * (512 + T)
            w = qt // 4
            u2 = idx % 2
            ks_ = f"f{l}cS{u2}"
            for half in range(2):
                p.op("tensor", lambda e, half=half, u2=u2, hb=hb, qrow=qrow,
                     krow=krow, w=w, qt=qt: e.matmul(
                    psS[u2][:, 512 * half:512 * (half + 1)],
                    qp[hb:hb + 64, qrow + 128 * qt:qrow + 128 * (qt + 1)],
                    kp[hb:hb + 64, krow + 512 * w + 512 * half:
                       krow + 512 * w + 512 * (half + 1)],
                    start=True, stop=True),
                    reads=("qp", "kp"), writes=(ks_,))

        def c_soft(idx):
            h, qt = items[idx]
            w = qt // 4
            u = idx % NU
            u2 = idx % 2
            ks_ = f"f{l}cS{u2}"
            m = (4 if w == 0 else 0) + qt % 4
            p.op("vector", lambda e, u=u, u2=u2, m=m: e.tensor_tensor(
                ssb[u][:], psS[u2][:, :],
                masks[:, 1024 * m:1024 * (m + 1)], ALU.add),
                reads=(ks_, "masks"), writes=(f"f{l}cs{u}",))
            p.op("vector", lambda e, u=u: e.reduce_max(
                mx[u][:], ssb[u][:], axis=AX.X, negate=True),
                reads=(f"f{l}cs{u}",), writes=(f"f{l}cm{u}",))
            p.op("scalar", lambda e, u=u: e.activation(
                abf[u][:], ssb[u][:], AF.Exp, bias=mx[u][:],
                accum_out=sm[u][:]),
                reads=(f"f{l}cs{u}", f"f{l}cm{u}"),
                writes=(f"f{l}ca{u}", f"f{l}csum{u}"))
            p.op("vector", lambda e, u=u: e.reciprocal(sm[u][:], sm[u][:]),
                 reads=(f"f{l}csum{u}",), writes=(f"f{l}crs{u}",))

        def c_rest(idx):
            h, qt = items[idx]
            hb = 64 * (h % 2)
            qrow = (h // 2) * T
            w = qt // 4
            u = idx % NU
            u2 = idx % 2
            for j in range(8):
                p.op("tensor", lambda e, u=u, u2=u2, j=j: e.transpose(
                    psT[u2][:, 128 * j:128 * (j + 1)],
                    abf[u][:, 128 * j:128 * (j + 1)], id128[:]),
                    reads=(f"f{l}ca{u}", "id128"), writes=(f"f{l}cT{u2}",))
            if idx % 2:
                p.op("vector", lambda e, u=u, u2=u2: e.tensor_copy(
                    atT[u][:], psT[u2][:, :]),
                    reads=(f"f{l}cT{u2}",), writes=(f"f{l}cat{u}",))
            else:
                p.op("scalar", lambda e, u=u, u2=u2: e.copy(
                    atT[u][:], psT[u2][:, :]),
                    reads=(f"f{l}cT{u2}",), writes=(f"f{l}cat{u}",))
            jcs = [j for j in range(8) if 4 * (w - 1) + j >= 0]
            for idx2, j in enumerate(jcs):
                kc = 4 * (w - 1) + j
                p.op("tensor", lambda e, u=u, j=j, kc=kc, h=h,
                     st=(idx2 == 0), sp=(idx2 == len(jcs) - 1): e.matmul(
                    psO[:, 128 * u:128 * u + 64],
                    atT[u][:, 128 * j:128 * (j + 1)],
                    vtok[:, 512 * kc + 64 * h:512 * kc + 64 * (h + 1)],
                    start=st, stop=sp),
                    reads=(f"f{l}cat{u}", "vtok"), writes=(f"f{l}cO{u}",))
            p.op("vector", lambda e, u=u, qt=qt: e.tensor_scalar(
                otm[:, 64 * qt:64 * (qt + 1)], psO[:, 128 * u:128 * u + 64],
                sm[u][:], None, ALU.mult),
                reads=(f"f{l}cO{u}", f"f{l}crs{u}"), writes=("otm",))
            p.op("tensor", lambda e, qt=qt, hb=hb: e.transpose(
                psB2[hb:hb + 64, 128 * (qt % 8):128 * (qt % 8) + 128],
                otm[:, 64 * qt:64 * (qt + 1)], id128[:],
                tile_position=(0, hb)),
                reads=("otm", "id128"), writes=(f"f{l}cB",))
            if qt % 8 == 7:
                p.op("vector", lambda e, qt=qt, hb=hb, qrow=qrow:
                     e.tensor_copy(
                    ofm[hb:hb + 64, qrow + 128 * (qt - 7):
                        qrow + 128 * (qt + 1)], psB2[hb:hb + 64, :]),
                    reads=(f"f{l}cB",), writes=("ofm",))

        N = len(items)
        c_scores(0)
        for i in range(N):
            c_soft(i)
            if i + 1 < N:
                c_scores(i + 1)
            c_rest(i)
        p.barrier()
        sc.close()

    if "D" in stages:
        sc = contextlib.ExitStack()
        wo_sb = wload(sc, f"f{l}_wo", d[f"wo{l}"], 4, 1024, parts=4)
        ztd = p.sb_in(sc, f"f{l}d_z", [128, 64], F32)
        p.op("vector", lambda e: e.memset(ztd[:], 0.0), writes=("ztd",))
        p.op("sync", lambda e: e.dma_start(
            out=ymid[:, 0:8].rearrange("(c p) t -> p c t", p=128),
            in_=ztd[:].rearrange("p (c t) -> p c t", t=8)),
            reads=("ztd",), writes=("hbm.ymid.z",), dma=True)
        ytd = [p.sb_in(sc, f"f{l}d_y{i}", [128, TT], F32) for i in range(2)]
        yin_t = [p.sb_in(sc, f"f{l}d_yi{i}", [128, TT],
                         F32 if yin_f32 else BF16) for i in range(2)]
        psdd = [p.ps_in(sc, f"f{l}d_ps{i}", [128, TT]) for i in range(2)]
        for i in range(8):
            t0 = TT * i
            for oc in range(8):
                u = (i * 8 + oc) % 2
                p.op("sync", lambda e, oc=oc, t0=t0, u=u: e.dma_start(
                    out=yin_t[u][:],
                    in_=yin[128 * oc:128 * (oc + 1), 8 + t0:8 + t0 + TT]),
                    writes=(f"f{l}dyi{u}",), dma=True)
                for ic in range(4):
                    p.op("tensor", lambda e, oc=oc, ic=ic, t0=t0, u=u: e.matmul(
                        psdd[u][:, :],
                        wo_sb[:, 1024 * ic + 128 * oc:1024 * ic + 128 * (oc + 1)],
                        ofm[:, ic * T + t0:ic * T + t0 + TT],
                        start=(ic == 0), stop=(ic == 3)),
                        reads=("ofm", f"f{l}_wo"), writes=(f"f{l}dps{u}",))
                p.op("vector", lambda e, u=u: e.tensor_tensor(
                    ytd[u][:], psdd[u][:, :], yin_t[u][:], ALU.add),
                    reads=(f"f{l}dps{u}", f"f{l}dyi{u}"), writes=(f"f{l}dy{u}",))
                p.op("sync", lambda e, oc=oc, t0=t0, u=u: e.dma_start(
                    out=ymid[128 * oc:128 * (oc + 1), 8 + t0:8 + t0 + TT],
                    in_=ytd[u][:]),
                    reads=(f"f{l}dy{u}",), writes=("hbm.ymid",), dma=True)
        p.barrier()
        sc.close()
    scCD.close()
    scAD.close()

    if "E" in stages:
        sc = contextlib.ExitStack()
        lng = p.sb_in(sc, f"f{l}_lng", [128, 8], F32)
        lnb = p.sb_in(sc, f"f{l}_lnb", [128, 8], F32)
        p.op("sync", lambda e: e.dma_start(out=lng[:], in_=d[f"lng{l}"]),
             writes=("lng",), dma=True)
        p.op("sync", lambda e: e.dma_start(out=lnb[:], in_=d[f"lnb{l}"]),
             writes=("lnb",), dma=True)
        lgT = p.sb_in(sc, f"f{l}_lgT", [2, 8 * 128], F32)
        p.op("sync", lambda e: e.dma_start(out=lgT[:], in_=d[f"lngT{l}"]),
             writes=("lgT",), dma=True)
        xte = [p.sb_in(sc, f"f{l}e_x{i}", [128, 8 * TT], F32) for i in range(2)]
        sqte = [p.sb_in(sc, f"f{l}e_sq{i}", [128, 8 * TT], F32) for i in range(2)]
        lnoe = [p.sb_in(sc, f"f{l}e_o{i}", [128, 8 * TT], BF16) for i in range(2)]
        tmpe = [p.sb_in(sc, f"f{l}e_t{i}", [128, TT], F32) for i in range(2)]
        rowse = [p.sb_in(sc, f"f{l}e_rows{i}", [2, 4 * TT], F32) for i in range(2)]
        for bb_ in range(2):
            p.op("sync", lambda e, bb_=bb_: e.dma_start(
                out=rowse[bb_][1:2, 3 * TT:4 * TT], in_=d["c_ones512"]),
                writes=(f"eones{bb_}",), dma=True)
        psAe = [p.ps_in(sc, f"f{l}e_psA{i}", [1, TT]) for i in range(2)]
        psBe = [p.ps_in(sc, f"f{l}e_psB{i}", [1, TT]) for i in range(2)]
        psBCe = [p.ps_in(sc, f"f{l}e_psBC{i}", [128, 2 * TT]) for i in range(2)]
        def e_head(i):
            b = i % 2
            t0 = TT * i
            kx = f"f{l}e_x{b}"
            for half in range(2):
                sh = 1 if half == 1 else 0
                p.op("sync", lambda e, half=half, sh=sh, t0=t0, b=b: e.dma_start(
                    out=xte[b][:, half * 4 * TT:(half + 1) * 4 * TT]
                    .rearrange("p (c t) -> p c t", t=TT),
                    in_=_xtile_hbm(ymid, 4 * half, 4, t0 - sh, TT, pad=8)),
                    reads=("hbm.ymid", "hbm.ymid.z"), writes=(kx,), dma=True)
            p.op("scalar", lambda e, b=b: e.activation(
                sqte[b][:], xte[b][:], AF.Square),
                reads=(kx,), writes=(f"f{l}esq{b}",))
            for c in range(8):
                p.op("tensor", lambda e, b=b, c=c: e.matmul(
                    psAe[b][:, :], onescol[:], xte[b][:, TT * c:TT * (c + 1)],
                    start=(c == 0), stop=(c == 7)),
                    reads=(kx, "onescol"), writes=(f"psAe{b}",))
            for c in range(8):
                p.op("tensor", lambda e, b=b, c=c: e.matmul(
                    psBe[b][:, :], onescol[:], sqte[b][:, TT * c:TT * (c + 1)],
                    start=(c == 0), stop=(c == 7)),
                    reads=(f"f{l}esq{b}", "onescol"), writes=(f"psBe{b}",))

        def e_tail(i):
            b = i % 2
            t0 = TT * i
            kx = f"f{l}e_x{b}"
            m_ = rowse[b][0:1, 0:TT]
            v_ = rowse[b][0:1, TT:2 * TT]
            a_ = rowse[b][0:1, 2 * TT:3 * TT]
            bon = rowse[b][0:2, 3 * TT:4 * TT]
            p.op("vector", lambda e, b=b, m_=m_: e.tensor_scalar(
                m_, psAe[b][:, :], 1.0 / D, None, ALU.mult),
                reads=(f"psAe{b}",), writes=(f"em{b}",))
            p.op("vector", lambda e, b=b, m_=m_, v_=v_: e.scalar_tensor_tensor(
                v_, m_, -1.0, m_, ALU.mult, ALU.mult),
                reads=(f"em{b}",), writes=(f"ev{b}",))
            p.op("vector", lambda e, b=b, v_=v_: e.scalar_tensor_tensor(
                v_, psBe[b][:, :], 1.0 / D, v_, ALU.mult, ALU.add),
                reads=(f"psBe{b}", f"ev{b}"), writes=(f"ev{b}",))
            p.op("vector", lambda e, b=b, v_=v_: e.tensor_scalar(
                v_, v_, 1e-5, None, ALU.add),
                reads=(f"ev{b}",), writes=(f"ev{b}",))
            p.op("scalar", lambda e, b=b, v_=v_: e.activation(v_, v_, AF.Ln),
                 reads=(f"ev{b}",), writes=(f"ev{b}",))
            p.op("scalar", lambda e, b=b, v_=v_, a_=a_: e.activation(
                a_, v_, AF.Exp, scale=-0.5),
                reads=(f"ev{b}",), writes=(f"ea{b}",))
            p.op("vector", lambda e, b=b, m_=m_, a_=a_, bon=bon:
                 e.scalar_tensor_tensor(
                bon[0:1, :], m_, -1.0, a_, ALU.mult, ALU.mult),
                reads=(f"em{b}", f"ea{b}"), writes=(f"eb{b}",))
            for c in range(8):
                p.op("tensor", lambda e, b=b, c=c, a_=a_: e.matmul(
                    psBCe[b][:, 0:TT], lgT[0:1, 128 * c:128 * (c + 1)],
                    a_, start=True, stop=True),
                    reads=(f"ea{b}", "lgT"), writes=(f"ebca{b}",))
                p.op("tensor", lambda e, b=b, c=c, bon=bon: e.matmul(
                    psBCe[b][:, TT:2 * TT], lgT[:, 128 * c:128 * (c + 1)],
                    bon, start=True, stop=True),
                    reads=(f"eb{b}", f"eones{b}", "lgT"), writes=(f"ebcb{b}",))
                tb = c % 2
                p.op("vector", lambda e, b=b, c=c, tb=tb: e.tensor_tensor(
                    tmpe[tb][:], xte[b][:, TT * c:TT * (c + 1)],
                    psBCe[b][:, 0:TT], ALU.mult),
                    reads=(kx, f"ebca{b}"), writes=(f"f{l}et{tb}",))
                p.op("vector", lambda e, b=b, c=c, tb=tb: e.tensor_tensor(
                    lnoe[b][:, TT * c:TT * (c + 1)], tmpe[tb][:],
                    psBCe[b][:, TT:2 * TT], ALU.add),
                    reads=(f"f{l}et{tb}", f"ebcb{b}"), writes=(f"f{l}eo{b}",))
            p.op("sync", lambda e, t0=t0, b=b: e.dma_start(
                out=_xtile_hbm(lnh, 0, 8, t0, TT, pad=0),
                in_=lnoe[b][:].rearrange("p (c t) -> p c t", t=TT)),
                reads=(f"f{l}eo{b}",), writes=("hbm.lnh",), dma=True)

        e_head(0)
        for i in range(8):
            if i + 1 < 8:
                e_head(i + 1)
            e_tail(i)
        p.barrier()
        sc.close()

    if "F" in stages:
        sc = contextlib.ExitStack()
        w1_sb = wload(sc, f"f{l}_w1", d[f"w1{l}"], 8, 5632)
        b1a = p.sb_in(sc, f"f{l}_b1a", [128, 22], F32)
        b1g = p.sb_in(sc, f"f{l}_b1g", [128, 22], F32)
        p.op("sync", lambda e: e.dma_start(out=b1a[:], in_=d[f"b1a{l}"]),
             writes=("b1a",), dma=True)
        p.op("sync", lambda e: e.dma_start(out=b1g[:], in_=d[f"b1g{l}"]),
             writes=("b1g",), dma=True)
        xtf = [p.sb_in(sc, f"f{l}f_x{i}", [128, 8 * TT], BF16) for i in range(2)]
        gelf = [p.sb_in(sc, f"f{l}f_g{i}", [128, TT], F32) for i in range(2)]
        avf = [p.sb_in(sc, f"f{l}f_a{i}", [128, TT], F32) for i in range(2)]
        hgof = [p.sb_in(sc, f"f{l}f_o{i}", [128, TT], BF16) for i in range(2)]
        psaf = [p.ps_in(sc, f"f{l}f_pa{i}", [128, TT]) for i in range(2)]
        psgf = [p.ps_in(sc, f"f{l}f_pg{i}", [128, TT]) for i in range(2)]
        for i in range(8):
            b = i % 2
            t0 = TT * i
            kx = f"f{l}f_x{b}"
            p.op("sync", lambda e, t0=t0, b=b: e.dma_start(
                out=xtf[b][:].rearrange("p (c t) -> p c t", t=TT),
                in_=_xtile_hbm(lnh, 0, 8, t0, TT, pad=0)),
                reads=("hbm.lnh",), writes=(kx,), dma=True)
            for j in range(22):
                u = j % 2
                for c in range(8):
                    p.op("tensor", lambda e, b=b, j=j, c=c, u=u: e.matmul(
                        psaf[u][:, :], w1_sb[:, 5632 * c + 128 * j:
                                            5632 * c + 128 * (j + 1)],
                        xtf[b][:, TT * c:TT * (c + 1)],
                        start=(c == 0), stop=(c == 7)),
                        reads=(kx, f"f{l}_w1"), writes=(f"f{l}fpa{u}",))
                for c in range(8):
                    p.op("tensor", lambda e, b=b, j=j, c=c, u=u: e.matmul(
                        psgf[u][:, :], w1_sb[:, 5632 * c + 2816 + 128 * j:
                                            5632 * c + 2816 + 128 * (j + 1)],
                        xtf[b][:, TT * c:TT * (c + 1)],
                        start=(c == 0), stop=(c == 7)),
                        reads=(kx, f"f{l}_w1"), writes=(f"f{l}fpg{u}",))
                p.op("scalar", lambda e, j=j, u=u: e.activation(
                    gelf[u][:], psgf[u][:, :], AF.Gelu, bias=b1g[:, j:j + 1]),
                    reads=(f"f{l}fpg{u}", "b1g"), writes=(f"f{l}fg{u}",))
                p.op("vector", lambda e, j=j, u=u: e.tensor_scalar(
                    avf[u][:], psaf[u][:, :], b1a[:, j:j + 1], None, ALU.add),
                    reads=(f"f{l}fpa{u}", "b1a"), writes=(f"f{l}fa{u}",))
                p.op("vector", lambda e, u=u: e.tensor_tensor(
                    hgof[u][:], avf[u][:], gelf[u][:], ALU.mult),
                    reads=(f"f{l}fa{u}", f"f{l}fg{u}"), writes=(f"f{l}fo{u}",))
                p.op("sync", lambda e, j=j, t0=t0, u=u: e.dma_start(
                    out=hg[128 * j:128 * (j + 1), t0:t0 + TT], in_=hgof[u][:]),
                    reads=(f"f{l}fo{u}",), writes=("hbm.hg",), dma=True)
        p.barrier()
        sc.close()

    if "G" in stages:
        sc = contextlib.ExitStack()
        w2_sb = wload(sc, f"f{l}_w2", d[f"w2{l}"], 22, 1024, parts=22)
        b2 = p.sb_in(sc, f"f{l}_b2", [128, 8], F32)
        p.op("sync", lambda e: e.dma_start(out=b2[:], in_=d[f"b2{l}"]),
             writes=("b2",), dma=True)
        ypad = yout.shape[1] > T
        if ypad:
            ztg = p.sb_in(sc, f"f{l}g_z", [128, 64], F32)
            p.op("vector", lambda e: e.memset(ztg[:], 0.0), writes=("ztg",))
            p.op("sync", lambda e: e.dma_start(
                out=yout[:, 0:8].rearrange("(c p) t -> p c t", p=128),
                in_=ztg[:].rearrange("p (c t) -> p c t", t=8)),
                reads=("ztg",), writes=("hbm.yout.z",), dma=True)
        off = 8 if ypad else 0
        xtg = [p.sb_in(sc, f"f{l}g_x{i}", [128, 22 * TT], BF16) for i in range(2)]
        ymtg = [p.sb_in(sc, f"f{l}g_ym{i}", [128, TT], F32) for i in range(2)]
        yog = [p.sb_in(sc, f"f{l}g_o{i}", [128, TT], F32) for i in range(2)]
        psdg = [p.ps_in(sc, f"f{l}g_ps{i}", [128, TT]) for i in range(2)]
        for i in range(8):
            b = i % 2
            t0 = TT * i
            kx = f"f{l}g_x{b}"
            p.op("sync", lambda e, t0=t0, b=b: e.dma_start(
                out=xtg[b][:].rearrange("p (c t) -> p c t", t=TT),
                in_=hg.rearrange("(c p) t -> p c t", p=128)[:, :, t0:t0 + TT]),
                reads=("hbm.hg",), writes=(kx,), dma=True)
            for oc in range(8):
                u = (i * 8 + oc) % 2
                p.op("sync", lambda e, oc=oc, t0=t0, u=u: e.dma_start(
                    out=ymtg[u][:],
                    in_=ymid[128 * oc:128 * (oc + 1), 8 + t0:8 + t0 + TT]),
                    reads=("hbm.ymid",), writes=(f"f{l}gym{u}",), dma=True)
                for ic in range(22):
                    p.op("tensor", lambda e, oc=oc, ic=ic, u=u, b=b: e.matmul(
                        psdg[u][:, :],
                        w2_sb[:, 1024 * ic + 128 * oc:1024 * ic + 128 * (oc + 1)],
                        xtg[b][:, TT * ic:TT * (ic + 1)],
                        start=(ic == 0), stop=(ic == 21)),
                        reads=(kx, f"f{l}_w2"), writes=(f"f{l}gps{u}",))
                p.op("vector", lambda e, oc=oc, u=u: e.scalar_tensor_tensor(
                    yog[u][:], psdg[u][:, :], b2[:, oc:oc + 1], ymtg[u][:],
                    ALU.add, ALU.add),
                    reads=(f"f{l}gps{u}", "b2", f"f{l}gym{u}"),
                    writes=(f"f{l}go{u}",))
                p.op("sync", lambda e, oc=oc, t0=t0, u=u: e.dma_start(
                    out=yout[128 * oc:128 * (oc + 1), off + t0:off + t0 + TT],
                    in_=yog[u][:]),
                    reads=(f"f{l}go{u}",), writes=("hbm.yout",), dma=True)
        p.barrier()
        sc.close()


def phase_ffn(l, stages="ACDEFG"):
    yin = "xnz" if l == 0 else "t_y2"
    yout = "t_y2" if l == 0 else "t_y4"
    specs = [
        ("xnz", (D, T + 8), BF16, "ExternalInput"),
        (f"qkv{l}", (D, 1536), F32, "ExternalInput"),
        (f"wo{l}", (512, D), F32, "ExternalInput"),
        (f"w1{l}", (D, 5632), F32, "ExternalInput"),
        (f"b1a{l}", (128, 22), F32, "ExternalInput"),
        (f"b1g{l}", (128, 22), F32, "ExternalInput"),
        (f"w2{l}", (2816, D), F32, "ExternalInput"),
        (f"b2{l}", (128, 8), F32, "ExternalInput"),
        (f"lng{l}", (128, 8), F32, "ExternalInput"),
        (f"lnb{l}", (128, 8), F32, "ExternalInput"),
        (f"lngT{l}", (2, 1024), F32, "ExternalInput"),
        (f"qs{l}", (128, 1), F32, "ExternalInput"),
        (f"ks{l}", (128, 1), F32, "ExternalInput"),
        ("c_ones1", (1, 128), F32, "ExternalInput"),
        ("c_onescol", (128, 1), F32, "ExternalInput"),
        ("c_ones512", (1, 512), F32, "ExternalInput"),
        ("c_blockones", (128, 2), F32, "ExternalInput"),
        ("c_sel2", (2, 128), F32, "ExternalInput"),
        ("c_id128", (128, 128), BF16, "ExternalInput"),
        ("c_rot", (128, 128), BF16, "ExternalInput"),
        ("c_csq", (128, T), BF16, "ExternalInput"),
        ("c_ssq", (128, T), BF16, "ExternalInput"),
        ("c_csk", (128, T), BF16, "ExternalInput"),
        ("c_ssk", (128, T), BF16, "ExternalInput"),
        ("c_attnmask", (1024, 1024), BF16, "ExternalInput"),
        ("t_lnh", (D, T), BF16, "Internal"),
        ("t_hg", (2816, T), BF16, "Internal"),
        ("t_ymid", (D, T + 8), F32, "Internal"),
    ]
    ins = [s[0] for s in specs if s[3] == "ExternalInput"]
    if l == 0:
        specs.append(("t_y2", (D, T + 8), F32, "ExternalOutput"))
        outs = ["t_y2"]
    else:
        specs.append(("t_y2", (D, T + 8), F32, "ExternalInput"))
        specs.append(("t_y4", (D, T), F32, "ExternalOutput"))
        ins.append("t_y2")
        outs = ["t_y4"]
    yin_f32 = (l == 1)

    def emitter(p, d):
        emit_ffn(p, d, l, d[yin], yin_f32, d[yout], stages=stages)
    return _mk_nc(emitter, specs), ins, outs




def emit_p4(p, d):
    TT = 512
    x = d["x_fm"]; s0 = d["s0"]; y4 = d["t_y4"]
    x1 = d["t_x1"]; xn1 = d["t_xn1"]
    ones1 = p.sb("p4_ones1", [1, 128], F32)
    p.op("sync", lambda e: e.dma_start(out=ones1[:], in_=d["c_ones1"]),
         writes=("ones1",), dma=True)
    onescol = p.sb("p4_ones", [128, 1], F32)
    p.op("sync", lambda e: e.dma_start(out=onescol[:], in_=d["c_onescol"]),
         writes=("onescol",), dma=True)
    xt = [p.sb(f"p4_x{i}", [128, 8 * TT], F32) for i in range(2)]
    st = [p.sb(f"p4_s{i}", [128, 8 * TT], F32) for i in range(2)]
    yt = [p.sb(f"p4_y{i}", [128, 8 * TT], F32) for i in range(2)]
    gt = yt
    x1t = [p.sb(f"p4_o{i}", [128, 8 * TT], F32) for i in range(2)]
    sqt = st
    xno = [p.sb(f"p4_n{i}", [128, 8 * TT], BF16) for i in range(2)]
    tmp = [p.sb(f"p4_t{i}", [128, TT], F32) for i in range(2)]
    rows = p.sb("p4_rows", [1, 6 * TT], F32)
    psA = p.ps("p4_psA", [1, TT])
    psB = p.ps("p4_psB", [1, TT])
    psBC = p.ps("p4_psBC", [128, 2 * TT])
    for i in range(8):
        b = i % 2
        t0 = TT * i
        for nm, buf, src_ in (("x", xt, x), ("s", st, s0)):
            p.op("sync", lambda e, buf=buf, src_=src_, b=b, t0=t0: e.dma_start(
                out=buf[b][:].rearrange("p (c t) -> p c t", t=TT),
                in_=_xtile_hbm(src_, 0, 8, t0, TT)),
                writes=(f"p4{nm}{b}",), dma=True)
        p.op("sync", lambda e, b=b, t0=t0: e.dma_start(
            out=yt[b][:].rearrange("p (c t) -> p c t", t=TT),
            in_=_xtile_hbm(y4, 0, 8, t0, TT)),
            writes=(f"p4y{b}",), dma=True)
        p.op("scalar", lambda e, b=b: e.activation(gt[b][:], yt[b][:], AF.Silu),
             reads=(f"p4y{b}",), writes=(f"p4y{b}",))
        p.op("vector", lambda e, b=b: e.tensor_tensor(
            gt[b][:], gt[b][:], st[b][:], ALU.mult),
            reads=(f"p4y{b}", f"p4s{b}"), writes=(f"p4y{b}",))
        p.op("vector", lambda e, b=b: e.tensor_tensor(
            x1t[b][:], xt[b][:], gt[b][:], ALU.add),
            reads=(f"p4x{b}", f"p4y{b}"), writes=(f"p4o{b}",))
        p.op("sync", lambda e, b=b, t0=t0: e.dma_start(
            out=_xtile_hbm(x1, 0, 8, t0, TT),
            in_=x1t[b][:].rearrange("p (c t) -> p c t", t=TT)),
            reads=(f"p4o{b}",), writes=("hbm.x1",), dma=True)
        p.op("scalar", lambda e, b=b: e.activation(sqt[b][:], x1t[b][:], AF.Square),
             reads=(f"p4o{b}",), writes=(f"p4s{b}",))
        for c in range(8):
            p.op("tensor", lambda e, b=b, c=c: e.matmul(
                psA[:, :], onescol[:], x1t[b][:, TT * c:TT * (c + 1)],
                start=(c == 0), stop=(c == 7)),
                reads=(f"p4o{b}", "onescol"), writes=("psA",))
        for c in range(8):
            p.op("tensor", lambda e, b=b, c=c: e.matmul(
                psB[:, :], onescol[:], sqt[b][:, TT * c:TT * (c + 1)],
                start=(c == 0), stop=(c == 7)),
                reads=(f"p4s{b}", "onescol"), writes=("psB",))
        m_ = rows[:, 0:TT]; msq = rows[:, TT:2 * TT]; v_ = rows[:, 2 * TT:3 * TT]
        a_ = rows[:, 3 * TT:4 * TT]; b_ = rows[:, 4 * TT:5 * TT]
        p.op("vector", lambda e: e.tensor_scalar(m_, psA[:, :], 1.0 / D, None,
             ALU.mult), reads=("psA",), writes=("p4m",))
        p.op("vector", lambda e: e.tensor_tensor(msq, m_, m_, ALU.mult),
             reads=("p4m",), writes=("p4msq",))
        p.op("vector", lambda e: e.scalar_tensor_tensor(
            v_, psB[:, :], 1.0 / D, msq, ALU.mult, ALU.subtract),
            reads=("psB", "p4msq"), writes=("p4v",))
        p.op("vector", lambda e: e.tensor_scalar(v_, v_, 1e-5, None, ALU.add),
             reads=("p4v",), writes=("p4v",))
        p.op("scalar", lambda e: e.activation(v_, v_, AF.Ln),
             reads=("p4v",), writes=("p4v",))
        p.op("scalar", lambda e: e.activation(a_, v_, AF.Exp, scale=-0.5),
             reads=("p4v",), writes=("p4a",))
        p.op("vector", lambda e: e.scalar_tensor_tensor(
            b_, m_, -1.0, a_, ALU.mult, ALU.mult),
            reads=("p4m", "p4a"), writes=("p4b",))
        p.op("tensor", lambda e: e.matmul(
            psBC[:, 0:TT], ones1[0:1, :], a_, start=True, stop=True),
            reads=("p4a", "ones1"), writes=("p4bca",))
        p.op("tensor", lambda e: e.matmul(
            psBC[:, TT:2 * TT], ones1[0:1, :], b_, start=True, stop=True),
            reads=("p4b", "ones1"), writes=("p4bcb",))
        for c in range(8):
            tb = c % 2
            p.op("vector", lambda e, b=b, c=c, tb=tb: e.tensor_tensor(
                tmp[tb][:], x1t[b][:, TT * c:TT * (c + 1)], psBC[:, 0:TT],
                ALU.mult), reads=(f"p4o{b}", "p4bca"), writes=(f"p4t{tb}",))
            p.op("vector", lambda e, b=b, c=c, tb=tb: e.tensor_tensor(
                xno[b][:, TT * c:TT * (c + 1)], tmp[tb][:], psBC[:, TT:2 * TT],
                ALU.add), reads=(f"p4t{tb}", "p4bcb"), writes=(f"p4n{b}",))
        p.op("sync", lambda e, b=b, t0=t0: e.dma_start(
            out=_xtile_hbm(xn1, 0, 8, t0, TT),
            in_=xno[b][:].rearrange("p (c t) -> p c t", t=TT)),
            reads=(f"p4n{b}",), writes=("hbm.xn1",), dma=True)


def phase_p4():
    specs = [
        ("x_fm", (D, T), F32, "ExternalInput"),
        ("s0", (D, T), F32, "ExternalInput"),
        ("t_y4", (D, T), F32, "ExternalInput"),
        ("c_ones1", (1, 128), F32, "ExternalInput"),
        ("c_onescol", (128, 1), F32, "ExternalInput"),
        ("t_x1", (D, T), F32, "ExternalOutput"),
        ("t_xn1", (D, T), BF16, "ExternalOutput"),
    ]
    ins = [s[0] for s in specs if s[3] == "ExternalInput"]
    return _mk_nc(emit_p4, specs), ins, ["t_x1", "t_xn1"]


def emit_p5(p, d):
    TT = 512
    xn1 = d["t_xn1"]; mq_h = d["t_mq"]; mk_h = d["t_mk"]; mqk_h = d["t_mqk"]
    s1 = d["s1"]
    selc = p.sb("p5_selc", [16, 1024], F32)
    p.op("sync", lambda e: e.dma_start(out=selc[:], in_=d["c_selc"]),
         writes=("selc",), dma=True)
    bqa = p.sb("p5_bqa", [16, 1], F32)
    p.op("sync", lambda e: e.dma_start(out=bqa[:], in_=d["bqa"]),
         writes=("bqa",), dma=True)
    bka = p.sb("p5_bka", [16, 1], F32)
    p.op("sync", lambda e: e.dma_start(out=bka[:], in_=d["bka"]),
         writes=("bka",), dma=True)
    bvec = {}
    for nm in ("bq", "bk", "bt"):
        bvec[nm] = p.sb(f"p5_{nm}", [128, 8], F32)
        p.op("sync", lambda e, nm=nm: e.dma_start(out=bvec[nm][:], in_=d[nm]),
             writes=(nm,), dma=True)
    qsc = p.sb("p5_qsc", [16, T], F32)
    ksc = p.sb("p5_ksc", [16, T], F32)
    eq = p.sb("p5_eq", [16, T], F32)
    mx16 = p.sb("p5_mx", [16, 1], F32)
    sm16 = p.sb("p5_sm", [16, 1], F32)
    pqs = p.sb("p5_pqs", [128, 8], F32)
    pks = p.sb("p5_pks", [128, 8], F32)
    acc = p.sb("p5_acc", [128, 8], F32)
    junk = p.sb("p5_junk", [128, T], F32)

    sc = contextlib.ExitStack()
    wq_sb = p.sb_in(sc, "p5_wq", [128, 8 * 1024], BF16)
    wk_sb = p.sb_in(sc, "p5_wk", [128, 8 * 1024], BF16)
    wqa_sb = p.sb_in(sc, "p5_wqa", [128, 8 * 16], BF16)
    wtmp = p.sb_in(sc, "p5_wtmp", [128, 2 * 1024], F32)
    for wnm, wsb, src_ in (("wq", wq_sb, "Wq"), ("wk", wk_sb, "Wk")):
        for g in range(0, 8, 2):
            p.op("sync", lambda e, g=g, src_=src_: e.dma_start(
                out=wtmp[:].rearrange("p (c i) -> p c i", i=1024),
                in_=d[src_][g * 128:(g + 2) * 128, :]
                .rearrange("(c p) i -> p c i", p=128)),
                writes=("wtmp",), dma=True)
            p.op("vector", lambda e, g=g, wsb=wsb: e.tensor_copy(
                wsb[:, g * 1024:(g + 2) * 1024], wtmp[:]),
                reads=("wtmp",), writes=(wnm,))
    p.op("sync", lambda e: e.dma_start(
        out=wtmp[:, 0:8 * 16].rearrange("p (c i) -> p c i", i=16),
        in_=d["Wqa"].rearrange("(c p) i -> p c i", p=128)),
        writes=("wtmp",), dma=True)
    p.op("vector", lambda e: e.tensor_copy(wqa_sb[:], wtmp[:, 0:8 * 16]),
         reads=("wtmp",), writes=("wqa",))
    xt = [p.sb_in(sc, f"p5a_x{i}", [128, 8 * TT], BF16) for i in range(2)]
    mqt = [p.sb_in(sc, f"p5a_mq{i}", [128, TT], BF16) for i in range(2)]
    psm = [p.ps_in(sc, f"p5a_ps{i}", [128, TT]) for i in range(2)]
    ps16 = p.ps_in(sc, "p5a_ps16", [16, TT])
    for i in range(8):
        b = i % 2
        t0 = TT * i
        kx = f"p5a_x{b}"
        p.op("sync", lambda e, b=b, t0=t0: e.dma_start(
            out=xt[b][:].rearrange("p (c t) -> p c t", t=TT),
            in_=_xtile_hbm(xn1, 0, 8, t0, TT)),
            writes=(kx,), dma=True)
        for wnm, wsb, bv, outh in (("wq", wq_sb, "bq", mq_h), ("wk", wk_sb, "bk", mk_h)):
            for oc in range(8):
                u = oc % 2
                for c in range(8):
                    p.op("tensor", lambda e, b=b, oc=oc, c=c, u=u, wsb=wsb: e.matmul(
                        psm[u][:, :],
                        wsb[:, 1024 * c + 128 * oc:1024 * c + 128 * (oc + 1)],
                        xt[b][:, TT * c:TT * (c + 1)],
                        start=(c == 0), stop=(c == 7)),
                        reads=(kx, wnm), writes=(f"p5ps{u}",))
                p.op("vector", lambda e, oc=oc, u=u, bv=bv: e.tensor_scalar(
                    mqt[u][:], psm[u][:, :], bvec[bv][:, oc:oc + 1], None,
                    ALU.add), reads=(f"p5ps{u}", bv), writes=(f"p5mq{u}",))
                p.op("sync", lambda e, oc=oc, t0=t0, u=u, outh=outh: e.dma_start(
                    out=outh[128 * oc:128 * (oc + 1), t0:t0 + TT], in_=mqt[u][:]),
                    reads=(f"p5mq{u}",), writes=("hbm." + ("mq" if outh is mq_h else "mk"),), dma=True)
    mqt2 = [p.sb_in(sc, f"p5b_x{i}", [128, 8 * TT], BF16) for i in range(2)]
    for i in range(8):
        b = i % 2
        t0 = TT * i
        kx = f"p5b_x{b}"
        p.op("sync", lambda e, b=b, t0=t0: e.dma_start(
            out=mqt2[b][:].rearrange("p (c t) -> p c t", t=TT),
            in_=_xtile_hbm(mq_h, 0, 8, t0, TT)),
            reads=("hbm.mq",), writes=(kx,), dma=True)
        for c in range(8):
            p.op("tensor", lambda e, b=b, c=c: e.matmul(
                ps16[:, :], wqa_sb[:, 16 * c:16 * (c + 1)],
                mqt2[b][:, TT * c:TT * (c + 1)],
                start=(c == 0), stop=(c == 7)),
                reads=(kx, "wqa"), writes=("p5ps16",))
        p.op("vector", lambda e, t0=t0: e.tensor_scalar(
            qsc[:, t0:t0 + TT], ps16[:, :], bqa[:], 0.125, ALU.add, ALU.mult),
            reads=("p5ps16", "bqa"), writes=("qsc",))
    p.barrier()
    sc.close()

    sc = contextlib.ExitStack()
    mqc = [p.sb_in(sc, f"p5d_x{i}", [128, T], BF16) for i in range(2)]
    bcf = p.sb_in(sc, "p5d_bc", [128, T], F32)
    psbc2 = [p.ps_in(sc, f"p5d_ps{i}", [128, TT]) for i in range(2)]
    pssm = p.ps_in(sc, "p5d_pssm", [128, 1])

    def pool(score_sb, kscore, src_h, khsrc, out_col, kout,
             mqc=None, psbc2=None, pssm=None):
        p.op("vector", lambda e: e.reduce_max(
            mx16[:], score_sb[:], axis=AX.X, negate=True),
            reads=(kscore,), writes=("mx16",))
        p.op("scalar", lambda e: e.activation(
            eq[:], score_sb[:], AF.Exp, bias=mx16[:], accum_out=sm16[:]),
            reads=(kscore, "mx16"), writes=("eq", "sm16"))
        p.op("vector", lambda e: e.reciprocal(sm16[:], sm16[:]),
             reads=("sm16",), writes=("sm16",))
        for c in range(8):
            b = c % 2
            kxc = f"p5d_x{b}"
            p.op("sync", lambda e, c=c, b=b, src_h=src_h, mqc=mqc: e.dma_start(
                out=mqc[b][:], in_=src_h[128 * c:128 * (c + 1), :]),
                reads=(khsrc,), writes=(kxc,), dma=True)
            for tt in range(8):
                u = tt % 2
                p.op("tensor", lambda e, c=c, tt=tt, u=u, psbc2=psbc2: e.matmul(
                    psbc2[u][:, :], selc[:, 128 * c:128 * (c + 1)],
                    eq[:, TT * tt:TT * (tt + 1)], start=True, stop=True),
                    reads=("eq", "selc"), writes=(kxc + f"ps{u}",))
                p.op("vector", lambda e, c=c, tt=tt, b=b, u=u, mqc=mqc,
                     psbc2=psbc2: e.scalar_tensor_tensor(
                    junk[:, TT * tt:TT * (tt + 1)],
                    mqc[b][:, TT * tt:TT * (tt + 1)], 1.0, psbc2[u][:, :],
                    ALU.mult, ALU.mult, accum_out=acc[:, tt:tt + 1]),
                    reads=(kxc, kxc + f"ps{u}"), writes=("junk", f"acc{tt}",))
            p.op("vector", lambda e, c=c: e.reduce_sum(
                acc[:, 0:1], acc[:, 0:8], axis=AX.X),
                reads=tuple(f"acc{tt}" for tt in range(8)), writes=("acc0r",))
            p.op("tensor", lambda e, c=c, pssm=pssm: e.matmul(
                pssm[:, :], selc[:, 128 * c:128 * (c + 1)], sm16[:],
                start=True, stop=True), reads=("sm16", "selc"), writes=(kxc + "sm",))
            p.op("vector", lambda e, c=c, out_col=out_col, pssm=pssm:
                 e.tensor_tensor(
                out_col[:, c:c + 1], acc[:, 0:1], pssm[:, :], ALU.mult),
                reads=("acc0r", kxc + "sm"), writes=(kout,))

    pool(qsc, "qsc", mq_h, "hbm.mq", pqs, "pqs", mqc, psbc2, pssm)
    p.barrier()
    sc.close()

    sc = contextlib.ExitStack()
    wka_sb = p.sb_in(sc, "p5_wka", [128, 8 * 16], BF16)
    wtmp2 = p.sb_in(sc, "p5_wtmp2", [128, 8 * 16], F32)
    p.op("sync", lambda e: e.dma_start(
        out=wtmp2[:].rearrange("p (c i) -> p c i", i=16),
        in_=d["Wka"].rearrange("(c p) i -> p c i", p=128)),
        writes=("wtmp2",), dma=True)
    p.op("vector", lambda e: e.tensor_copy(wka_sb[:], wtmp2[:]),
         reads=("wtmp2",), writes=("wka",))
    mkt = [p.sb_in(sc, f"p5e_x{i}", [128, 8 * TT], BF16) for i in range(2)]
    mqkt = [p.sb_in(sc, f"p5e_o{i}", [128, 8 * TT], BF16) for i in range(2)]
    ps16b = p.ps_in(sc, "p5e_ps16", [16, TT])
    for i in range(8):
        b = i % 2
        t0 = TT * i
        kx = f"p5e_x{b}"
        p.op("sync", lambda e, b=b, t0=t0: e.dma_start(
            out=mkt[b][:].rearrange("p (c t) -> p c t", t=TT),
            in_=_xtile_hbm(mk_h, 0, 8, t0, TT)),
            reads=("hbm.mk",), writes=(kx,), dma=True)
        for c in range(8):
            p.op("vector", lambda e, b=b, c=c: e.tensor_scalar(
                mqkt[b][:, TT * c:TT * (c + 1)], mkt[b][:, TT * c:TT * (c + 1)],
                pqs[:, c:c + 1], None, ALU.mult),
                reads=(kx, "pqs"), writes=(f"p5eo{b}",))
        for c in range(8):
            p.op("tensor", lambda e, b=b, c=c: e.matmul(
                ps16b[:, :], wka_sb[:, 16 * c:16 * (c + 1)],
                mqkt[b][:, TT * c:TT * (c + 1)],
                start=(c == 0), stop=(c == 7)),
                reads=(f"p5eo{b}", "wka"), writes=("p5ps16b",))
        p.op("vector", lambda e, t0=t0: e.tensor_scalar(
            ksc[:, t0:t0 + TT], ps16b[:, :], bka[:], 0.125, ALU.add, ALU.mult),
            reads=("p5ps16b", "bka"), writes=("ksc",))
        p.op("sync", lambda e, b=b, t0=t0: e.dma_start(
            out=_xtile_hbm(mqk_h, 0, 8, t0, TT),
            in_=mqkt[b][:].rearrange("p (c t) -> p c t", t=TT)),
            reads=(f"p5eo{b}",), writes=("hbm.mqk",), dma=True)
    p.barrier()
    sc.close()

    sc = contextlib.ExitStack()
    mqc = [p.sb_in(sc, f"p5f_x{i}", [128, T], BF16) for i in range(2)]
    psbc2 = [p.ps_in(sc, f"p5f_ps{i}", [128, TT]) for i in range(2)]
    pssm = p.ps_in(sc, "p5f_pssm", [128, 1])
    pool(ksc, "ksc", mqk_h, "hbm.mqk", pks, "pks", mqc, psbc2, pssm)
    p.barrier()
    sc.close()

    sc = contextlib.ExitStack()
    wt_sb = p.sb_in(sc, "p5_wt", [128, 8 * 1024], BF16)
    wtmp3 = p.sb_in(sc, "p5_wtmp3", [128, 2 * 1024], F32)
    for g in range(0, 8, 2):
        p.op("sync", lambda e, g=g: e.dma_start(
            out=wtmp3[:].rearrange("p (c i) -> p c i", i=1024),
            in_=d["Wt"][g * 128:(g + 2) * 128, :]
            .rearrange("(c p) i -> p c i", p=128)),
            writes=("wtmp3",), dma=True)
        p.op("vector", lambda e, g=g: e.tensor_copy(
            wt_sb[:, g * 1024:(g + 2) * 1024], wtmp3[:]),
            reads=("wtmp3",), writes=("wt",))
    mqg = [p.sb_in(sc, f"p5g_x{i}", [128, 8 * TT], BF16) for i in range(2)]
    wvg = [p.sb_in(sc, f"p5g_w{i}", [128, 8 * TT], BF16) for i in range(2)]
    s1o = [p.sb_in(sc, f"p5g_o{i}", [128, TT], F32) for i in range(2)]
    psg = [p.ps_in(sc, f"p5g_ps{i}", [128, TT]) for i in range(2)]
    for i in range(8):
        b = i % 2
        t0 = TT * i
        kx = f"p5g_x{b}"
        p.op("sync", lambda e, b=b, t0=t0: e.dma_start(
            out=mqg[b][:].rearrange("p (c t) -> p c t", t=TT),
            in_=_xtile_hbm(mq_h, 0, 8, t0, TT)),
            reads=("hbm.mq",), writes=(kx,), dma=True)
        for c in range(8):
            p.op("vector", lambda e, b=b, c=c: e.tensor_scalar(
                wvg[b][:, TT * c:TT * (c + 1)], mqg[b][:, TT * c:TT * (c + 1)],
                pks[:, c:c + 1], None, ALU.mult),
                reads=(kx, "pks"), writes=(f"p5gw{b}",))
        for oc in range(8):
            u = oc % 2
            for c in range(8):
                p.op("tensor", lambda e, b=b, oc=oc, c=c, u=u: e.matmul(
                    psg[u][:, :],
                    wt_sb[:, 1024 * c + 128 * oc:1024 * c + 128 * (oc + 1)],
                    wvg[b][:, TT * c:TT * (c + 1)],
                    start=(c == 0), stop=(c == 7)),
                    reads=(f"p5gw{b}", "wt"), writes=(f"p5gps{u}",))
            p.op("vector", lambda e, b=b, oc=oc, u=u: e.scalar_tensor_tensor(
                s1o[u][:], psg[u][:, :], bvec["bt"][:, oc:oc + 1],
                mqg[b][:, TT * oc:TT * (oc + 1)], ALU.add, ALU.add),
                reads=(f"p5gps{u}", "bt", kx), writes=(f"p5go{u}",))
            p.op("sync", lambda e, oc=oc, t0=t0, u=u: e.dma_start(
                out=s1[128 * oc:128 * (oc + 1), t0:t0 + TT], in_=s1o[u][:]),
                reads=(f"p5go{u}",), writes=("hbm.s1",), dma=True)
    p.barrier()
    sc.close()


def phase_p5():
    specs = [
        ("t_xn1", (D, T), BF16, "ExternalInput"),
        ("Wq", (D, D), F32, "ExternalInput"),
        ("Wk", (D, D), F32, "ExternalInput"),
        ("Wt", (D, D), F32, "ExternalInput"),
        ("Wqa", (D, 16), F32, "ExternalInput"),
        ("Wka", (D, 16), F32, "ExternalInput"),
        ("bq", (128, 8), F32, "ExternalInput"),
        ("bk", (128, 8), F32, "ExternalInput"),
        ("bt", (128, 8), F32, "ExternalInput"),
        ("bqa", (16, 1), F32, "ExternalInput"),
        ("bka", (16, 1), F32, "ExternalInput"),
        ("c_selc", (16, 1024), F32, "ExternalInput"),
        ("t_mq", (D, T), BF16, "Internal"),
        ("t_mk", (D, T), BF16, "Internal"),
        ("t_mqk", (D, T), BF16, "Internal"),
        ("s1", (D, T), F32, "ExternalOutput"),
    ]
    ins = [s[0] for s in specs if s[3] == "ExternalInput"]
    return _mk_nc(emit_p5, specs), ins, ["s1"]




def emit_p6_stage(p, d):
    wmoe = d["t_wmoe"]
    wsh = d["t_wsh"]
    for e_ in range(8):
        p.op("gpsimd", lambda e, e_=e_: e.dma_start(
            out=wmoe[e_ * 3456:e_ * 3456 + 1024, 0:1408], in_=d["wg"][e_]),
            writes=("hbm.wmoe",), dma=True)
        p.op("gpsimd", lambda e, e_=e_: e.dma_start(
            out=wmoe[e_ * 3456 + 1024:e_ * 3456 + 2048, 0:1408], in_=d["wu"][e_]),
            writes=("hbm.wmoe",), dma=True)
        p.op("gpsimd", lambda e, e_=e_: e.dma_start(
            out=wmoe[e_ * 3456 + 2048:e_ * 3456 + 3456, 0:1024], in_=d["wd"][e_]),
            writes=("hbm.wmoe",), dma=True)
    p.op("gpsimd", lambda e: e.dma_start(
        out=wsh[0:1024, 0:2816], in_=d["swg"]), writes=("hbm.wsh",), dma=True)
    p.op("gpsimd", lambda e: e.dma_start(
        out=wsh[1024:2048, 0:2816], in_=d["swu"]), writes=("hbm.wsh",), dma=True)
    p.op("gpsimd", lambda e: e.dma_start(
        out=wsh[2048:2048 + 2816, 0:1024], in_=d["swd"]),
        writes=("hbm.wsh",), dma=True)


def emit_p6(p, d):
    xn1 = d["t_xn1"]; x1 = d["t_x1"]; s1 = d["s1"]; out = d["out_fm"]
    wmoe = d["t_wmoe"]
    wsh = d["t_wsh"]
    oneh = p.sb("p6_oneh", [8, 1024], F32)
    p.op("sync", lambda e: e.dma_start(out=oneh[:], in_=d["c_oneh"]),
         writes=("oneh",), dma=True)
    idf = p.sb("p6_idf", [128, 128], F32)
    p.op("sync", lambda e: e.dma_start(out=idf[:], in_=d["c_id128f"]),
         writes=("idf",), dma=True)
    wfm = p.sb("p6_wfm", [8, T], F32)


    sc = contextlib.ExitStack()
    gw_sb = p.sb_in(sc, "p6_gw", [128, 8 * 8], BF16)
    gwf = p.sb_in(sc, "p6_gwf", [128, 8 * 8], F32)
    p.op("sync", lambda e: e.dma_start(
        out=gwf[:].rearrange("p (c i) -> p c i", i=8),
        in_=d["gwT"].rearrange("(c p) i -> p c i", p=128)),
        writes=("gwf",), dma=True)
    p.op("vector", lambda e: e.tensor_copy(gw_sb[:], gwf[:]),
         reads=("gwf",), writes=("gw",))
    xr = [p.sb_in(sc, f"p6r_x{i}", [128, 8 * 128], BF16) for i in range(2)]
    lg_ = p.sb_in(sc, "p6r_l", [128, 8], F32)
    m1p = p.sb_in(sc, "p6r_m1", [128, 1], F32)
    m1n = p.sb_in(sc, "p6r_m1n", [128, 1], F32)
    Ee = p.sb_in(sc, "p6r_E", [128, 8], F32)
    ind = p.sb_in(sc, "p6r_ind", [128, 8], F32)
    m2 = p.sb_in(sc, "p6r_m2", [128, 1], F32)
    den = p.sb_in(sc, "p6r_den", [128, 1], F32)
    wtok = p.sb_in(sc, "p6r_w", [128, 8], F32)
    psL = p.ps_in(sc, "p6r_psL", [128, 8])
    psWt = p.ps_in(sc, "p6r_psW", [8, 128])
    for i in range(32):
        b = i % 2
        kx = f"p6rx{b}"
        p.op("sync", lambda e, i=i, b=b: e.dma_start(
            out=xr[b][:].rearrange("p (c t) -> p c t", t=128),
            in_=_xtile_hbm(xn1, 0, 8, 128 * i, 128)),
            writes=(kx,), dma=True)
        for c in range(8):
            p.op("tensor", lambda e, b=b, c=c: e.matmul(
                psL[:, :], xr[b][:, 128 * c:128 * (c + 1)],
                gw_sb[:, 8 * c:8 * (c + 1)],
                start=(c == 0), stop=(c == 7)),
                reads=(kx, "gw"), writes=("psL",))
        p.op("vector", lambda e: e.tensor_copy(lg_[:], psL[:, :]),
             reads=("psL",), writes=("lg",))
        p.op("vector", lambda e: e.reduce_max(m1p[:], lg_[:], axis=AX.X),
             reads=("lg",), writes=("m1p",))
        p.op("vector", lambda e: e.tensor_scalar(m1n[:], m1p[:], -1.0, None,
             ALU.mult), reads=("m1p",), writes=("m1n",))
        p.op("scalar", lambda e: e.activation(Ee[:], lg_[:], AF.Exp,
             bias=m1n[:]), reads=("lg", "m1n"), writes=("Ee",))
        p.op("vector", lambda e: e.tensor_scalar(ind[:], lg_[:], m1p[:], None,
             ALU.is_ge), reads=("lg", "m1p"), writes=("ind",))
        p.op("vector", lambda e: e.scalar_tensor_tensor(
            ind[:], ind[:], -1e30, lg_[:], ALU.mult, ALU.add),
            reads=("ind", "lg"), writes=("ind",))
        p.op("vector", lambda e: e.reduce_max(m2[:], ind[:], axis=AX.X),
             reads=("ind",), writes=("m2",))
        p.op("scalar", lambda e: e.activation(den[:], m2[:], AF.Exp,
             bias=m1n[:]), reads=("m2", "m1n"), writes=("den",))
        p.op("vector", lambda e: e.tensor_scalar(den[:], den[:], 1.0, None,
             ALU.add), reads=("den",), writes=("den",))
        p.op("vector", lambda e: e.reciprocal(den[:], den[:]),
             reads=("den",), writes=("den",))
        p.op("vector", lambda e: e.tensor_scalar(wtok[:], lg_[:], m2[:], None,
             ALU.is_ge), reads=("lg", "m2"), writes=("wtok",))
        p.op("vector", lambda e: e.scalar_tensor_tensor(
            wtok[:], wtok[:], den[:], Ee[:], ALU.mult, ALU.mult),
            reads=("wtok", "den", "Ee"), writes=("wtok",))
        p.op("tensor", lambda e: e.transpose(psWt[:, :], wtok[:], idf[:]),
             reads=("wtok", "idf"), writes=("psWt",))
        p.op("vector", lambda e, i=i: e.tensor_copy(
            wfm[:, 128 * i:128 * (i + 1)], psWt[:, :]),
            reads=("psWt",), writes=("wfm",))
    p.barrier()
    sc.close()

    GT = 1024
    sc = contextlib.ExitStack()
    xg = p.sb_in(sc, "p6_xg", [128, 8 * GT], BF16)
    ysb = p.sb_in(sc, "p6_y", [128, 8 * GT], F32)
    wge = p.sb_in(sc, "p6_wge", [128, 8 * 1408], BF16)
    wue = p.sb_in(sc, "p6_wue", [128, 8 * 1408], BF16)
    wde = p.sb_in(sc, "p6_wde", [128, 11 * 1024], BF16)
    wbce = p.sb_in(sc, "p6_wbce", [128, GT], F32)
    sgl = p.sb_in(sc, "p6_sgl", [128, GT], F32)
    m1s = p.sb_in(sc, "p6_m1s", [128, GT], F32)
    m2s = p.sb_in(sc, "p6_m2s", [128, 11 * GT], BF16)
    x1t = p.sb_in(sc, "p6_x1t", [128, GT], F32)
    s1t = p.sb_in(sc, "p6_s1t", [128, GT], F32)
    g1t = p.sb_in(sc, "p6_g1t", [128, GT], F32)
    oto = p.sb_in(sc, "p6_oto", [128, GT], F32)
    psH = p.ps_in(sc, "p6_psH", [128, GT])
    psU = p.ps_in(sc, "p6_psU", [128, GT])
    psW2 = p.ps_in(sc, "p6_psW2", [128, GT])
    psDn = [p.ps_in(sc, f"p6_psD{i}", [128, 512]) for i in range(2)]

    def expert(g0, widx, nic, kw, wbc_on):
        for ic in range(nic):
            for half in range(2):
                for c in range(8):
                    p.op("tensor", lambda e, ic=ic, half=half, c=c: e.matmul(
                        psH[:, 512 * half:512 * (half + 1)],
                        wge[:, 1408 * c + 128 * ic:1408 * c + 128 * (ic + 1)],
                        xg[:, GT * c + 512 * half:GT * c + 512 * (half + 1)],
                        start=(c == 0), stop=(c == 7)),
                        reads=("p6xg", "p6wge"), writes=(f"psH{half}",))
                for c in range(8):
                    p.op("tensor", lambda e, ic=ic, half=half, c=c: e.matmul(
                        psU[:, 512 * half:512 * (half + 1)],
                        wue[:, 1408 * c + 128 * ic:1408 * c + 128 * (ic + 1)],
                        xg[:, GT * c + 512 * half:GT * c + 512 * (half + 1)],
                        start=(c == 0), stop=(c == 7)),
                        reads=("p6xg", "p6wue"), writes=(f"psU{half}",))
            p.op("scalar", lambda e, ic=ic: e.activation(
                sgl[:], psH[:, :], AF.Silu),
                reads=("psH0", "psH1"), writes=("sgl",))
            p.op("vector", lambda e, ic=ic: e.tensor_tensor(
                m1s[:], sgl[:], psU[:, :], ALU.mult),
                reads=("sgl", "psU0", "psU1"), writes=("m1s",))
            if wbc_on:
                p.op("vector", lambda e, ic=ic: e.tensor_tensor(
                    m2s[:, GT * ic:GT * (ic + 1)], m1s[:], wbce[:], ALU.mult),
                    reads=("m1s", "p6wbce"), writes=("m2s",))
            else:
                p.op("vector", lambda e, ic=ic: e.tensor_copy(
                    m2s[:, GT * ic:GT * (ic + 1)], m1s[:]),
                    reads=("m1s",), writes=("m2s",))
        for oc in range(8):
            u = oc % 2
            for half in range(2):
                for ic in range(nic):
                    p.op("tensor", lambda e, oc=oc, half=half, ic=ic, u=u,
                         nic=nic: e.matmul(
                        psDn[u][:, :],
                        wde[:, 1024 * ic + 128 * oc:1024 * ic + 128 * (oc + 1)],
                        m2s[:, GT * ic + 512 * half:GT * ic + 512 * (half + 1)],
                        start=(ic == 0), stop=(ic == nic - 1)),
                        reads=("m2s", "p6wde"), writes=(f"psD{u}",))
                p.op("vector", lambda e, oc=oc, half=half, u=u: e.tensor_tensor(
                    ysb[:, GT * oc + 512 * half:GT * oc + 512 * (half + 1)],
                    ysb[:, GT * oc + 512 * half:GT * oc + 512 * (half + 1)],
                    psDn[u][:, :], ALU.add),
                    reads=(f"psD{u}", "p6y"), writes=("p6y",))

    for g in range(4):
        g0 = GT * g
        p.op("sync", lambda e, g0=g0: e.dma_start(
            out=xg[:].rearrange("p (c t) -> p c t", t=GT),
            in_=_xtile_hbm(xn1, 0, 8, g0, GT)),
            writes=("p6xg",), dma=True)
        p.op("vector", lambda e: e.memset(ysb[:], 0.0), writes=("p6y",))
        for e_ in range(8):
            p.op("sync", lambda e, e_=e_: e.dma_start(
                out=wge[:].rearrange("p (c i) -> p c i", i=1408),
                in_=wmoe[e_ * 3456:e_ * 3456 + 1024, 0:1408]
                .rearrange("(c p) i -> p c i", p=128)),
                reads=("hbm.wmoe",), writes=("p6wge",), dma=True)
            p.op("sync", lambda e, e_=e_: e.dma_start(
                out=wue[:].rearrange("p (c i) -> p c i", i=1408),
                in_=wmoe[e_ * 3456 + 1024:e_ * 3456 + 2048, 0:1408]
                .rearrange("(c p) i -> p c i", p=128)),
                reads=("hbm.wmoe",), writes=("p6wue",), dma=True)
            p.op("sync", lambda e, e_=e_: e.dma_start(
                out=wde[:].rearrange("p (c i) -> p c i", i=1024),
                in_=wmoe[e_ * 3456 + 2048:e_ * 3456 + 2048 + 1408, 0:1024]
                .rearrange("(c p) i -> p c i", p=128)),
                reads=("hbm.wmoe",), writes=("p6wde",), dma=True)
            p.op("tensor", lambda e, e_=e_, g0=g0: e.matmul(
                psW2[:, 0:512], oneh[:, 128 * e_:128 * (e_ + 1)],
                wfm[:, g0:g0 + 512], start=True, stop=True),
                reads=("wfm", "oneh"), writes=("psW20",))
            p.op("tensor", lambda e, e_=e_, g0=g0: e.matmul(
                psW2[:, 512:1024], oneh[:, 128 * e_:128 * (e_ + 1)],
                wfm[:, g0 + 512:g0 + 1024], start=True, stop=True),
                reads=("wfm", "oneh"), writes=("psW21",))
            p.op("vector", lambda e: e.tensor_copy(wbce[:], psW2[:, :]),
                 reads=("psW20", "psW21"), writes=("p6wbce",))
            expert(g0, e_, 11, "", True)
        for sh in range(2):
            p.op("sync", lambda e, sh=sh: e.dma_start(
                out=wge[:].rearrange("p (c i) -> p c i", i=1408),
                in_=wsh[0:1024, 1408 * sh:1408 * (sh + 1)]
                .rearrange("(c p) i -> p c i", p=128)),
                reads=("hbm.wsh",), writes=("p6wge",), dma=True)
            p.op("sync", lambda e, sh=sh: e.dma_start(
                out=wue[:].rearrange("p (c i) -> p c i", i=1408),
                in_=wsh[1024:2048, 1408 * sh:1408 * (sh + 1)]
                .rearrange("(c p) i -> p c i", p=128)),
                reads=("hbm.wsh",), writes=("p6wue",), dma=True)
            p.op("sync", lambda e, sh=sh: e.dma_start(
                out=wde[:].rearrange("p (c i) -> p c i", i=1024),
                in_=wsh[2048 + 1408 * sh:2048 + 1408 * (sh + 1), 0:1024]
                .rearrange("(c p) i -> p c i", p=128)),
                reads=("hbm.wsh",), writes=("p6wde",), dma=True)
            expert(g0, 0, 11, "", False)
        for c in range(8):
            p.op("sync", lambda e, c=c, g0=g0: e.dma_start(
                out=x1t[:], in_=x1[128 * c:128 * (c + 1), g0:g0 + GT]),
                writes=("p6x1t",), dma=True)
            p.op("sync", lambda e, c=c, g0=g0: e.dma_start(
                out=s1t[:], in_=s1[128 * c:128 * (c + 1), g0:g0 + GT]),
                reads=("hbm.s1",) if "s1" in d else (), writes=("p6s1t",), dma=True)
            p.op("scalar", lambda e, c=c: e.activation(
                g1t[:], ysb[:, GT * c:GT * (c + 1)], AF.Silu),
                reads=("p6y",), writes=("p6g1t",))
            p.op("vector", lambda e: e.tensor_tensor(
                g1t[:], g1t[:], s1t[:], ALU.mult),
                reads=("p6g1t", "p6s1t"), writes=("p6g1t",))
            p.op("vector", lambda e: e.tensor_tensor(
                oto[:], x1t[:], g1t[:], ALU.add),
                reads=("p6x1t", "p6g1t"), writes=("p6oto",))
            p.op("sync", lambda e, c=c, g0=g0: e.dma_start(
                out=out[128 * c:128 * (c + 1), g0:g0 + GT], in_=oto[:]),
                reads=("p6oto",), writes=("hbm.out",), dma=True)
    p.barrier()
    sc.close()


def phase_p6():
    specs = [
        ("t_xn1", (D, T), BF16, "ExternalInput"),
        ("t_x1", (D, T), F32, "ExternalInput"),
        ("s1", (D, T), F32, "ExternalInput"),
        ("gwT", (D, 8), F32, "ExternalInput"),
        ("wg", (8, D, 1408), F32, "ExternalInput"),
        ("wu", (8, D, 1408), F32, "ExternalInput"),
        ("wd", (8, 1408, D), F32, "ExternalInput"),
        ("swg", (D, 2816), F32, "ExternalInput"),
        ("swu", (D, 2816), F32, "ExternalInput"),
        ("swd", (2816, D), F32, "ExternalInput"),
        ("c_oneh", (8, 1024), F32, "ExternalInput"),
        ("c_id128f", (128, 128), F32, "ExternalInput"),
        ("t_wmoe", (8 * 3456, 1408), BF16, "Internal"),
        ("t_wsh", (2048 + 2816, 2816), BF16, "Internal"),
        ("out_fm", (D, T), F32, "ExternalOutput"),
    ]
    ins = [s[0] for s in specs if s[3] == "ExternalInput"]

    def emitter(p, d):
        emit_p6_stage(p, d)
        p.barrier()
        emit_p6(p, d)
    return _mk_nc(emitter, specs), ins, ["out_fm"]


def _mk_nc(phase, specs):
    nc = bass.Bass("TRN2", target_bir_lowering=False)
    d = {}
    for nm, shape, dt, kind in specs:
        d[nm] = nc.dram_tensor(nm, list(shape), dt, kind=kind).ap()
    p = Prog(nc)
    phase(p, d)
    p.finalize()
    return nc


_CONST_SPECS = {
    "c_onescol": ((128, 1), F32), "c_ones1": ((1, 128), F32),
    "c_ones512": ((1, 512), F32),
    "c_id128": ((128, 128), BF16), "c_id128f": ((128, 128), F32),
    "c_blockones": ((128, 2), F32), "c_sel2": ((2, 128), F32),
    "c_sel16": ((16, 1024), F32), "c_selc": ((16, 1024), F32),
    "c_oneh": ((8, 1024), F32), "c_rot": ((128, 128), BF16),
    "c_mask01": ((64, 1024), BF16), "c_attnmask": ((1024, 1024), BF16),
    "c_csq": ((128, T), BF16), "c_ssq": ((128, T), BF16),
    "c_csk": ((128, T), BF16), "c_ssk": ((128, T), BF16),
}


def phase_p0():
    specs = [
        ("x_fm", (D, T), F32, "ExternalInput"),
        ("c_onescol", (128, 1), F32, "ExternalInput"),
        ("c_ones1", (1, 128), F32, "ExternalInput"),
        ("xnz", (D, T + 8), BF16, "ExternalOutput"),
    ]
    return _mk_nc(emit_p0, specs), ["x_fm", "c_onescol", "c_ones1"], ["xnz"]


def _run_phase(nc, in_names, out_names, state):
    in_maps = []
    for b in range(NCORE):
        m = {}
        for nm in in_names:
            v = state[nm]
            m[nm] = v[b] if isinstance(v, list) else v
        in_maps.append(m)
    res = run_bass_kernel_spmd(nc, in_maps, core_ids=list(range(NCORE)))
    for nm in out_names:
        state[nm] = [np.asarray(res.results[b][nm]) for b in range(NCORE)]




def _fused_specs():
    E, I, O = "ExternalInput", "Internal", "ExternalOutput"
    specs = [
        ("x_fm", (D, T), F32, E),
        ("xnz", (D, T + 8), BF16, I),
        ("s0", (D, T), F32, I),
        ("t_decperm", (64, 1024), F32, I),
        ("t_vb", (T, 1024), BF16, I),
        ("t_otok", (T, 1024), BF16, I),
        ("t_vraw", (T, 1024), F32, I),
        ("t_sdbg", (T, 1024), F32, I),
        ("t_bcb", (T, 1024), F32, I),
        ("t_lnh", (D, T), BF16, I),
        ("t_hg", (2816, T), BF16, I),
        ("t_ymid", (D, T + 8), F32, I),
        ("t_y2", (D, T + 8), F32, I),
        ("t_y4", (D, T), F32, I),
        ("t_x1", (D, T), F32, I),
        ("t_xn1", (D, T), BF16, I),
        ("t_mq", (D, T), BF16, I),
        ("t_mk", (D, T), BF16, I),
        ("t_mqk", (D, T), BF16, I),
        ("s1", (D, T), F32, I),
        ("t_wmoe", (8 * 3456, 1408), BF16, I),
        ("t_wsh", (2048 + 2816, 2816), BF16, I),
        ("out_fm", (D, T), F32, O),
        ("WwT", (D, D), F32, E), ("WgT", (D, 16), F32, E),
        ("WbT", (D, 16), F32, E), ("WaT", (D, 16), F32, E),
        ("WoutT", (D, D), F32, E),
        ("A_log", (16, 1), F32, E), ("dt_bias", (16, 1), F32, E),
        ("Wq", (D, D), F32, E), ("Wk", (D, D), F32, E), ("Wt", (D, D), F32, E),
        ("Wqa", (D, 16), F32, E), ("Wka", (D, 16), F32, E),
        ("bq", (128, 8), F32, E), ("bk", (128, 8), F32, E),
        ("bt", (128, 8), F32, E), ("bqa", (16, 1), F32, E),
        ("bka", (16, 1), F32, E),
        ("gwT", (D, 8), F32, E),
        ("wg", (8, D, 1408), F32, E), ("wu", (8, D, 1408), F32, E),
        ("wd", (8, 1408, D), F32, E),
        ("swg", (D, 2816), F32, E), ("swu", (D, 2816), F32, E),
        ("swd", (2816, D), F32, E),
    ]
    for l in range(2):
        specs += [
            (f"qkv{l}", (D, 1536), F32, E), (f"wo{l}", (512, D), F32, E),
            (f"w1{l}", (D, 5632), F32, E), (f"b1a{l}", (128, 22), F32, E),
            (f"b1g{l}", (128, 22), F32, E), (f"w2{l}", (2816, D), F32, E),
            (f"b2{l}", (128, 8), F32, E), (f"lng{l}", (128, 8), F32, E),
            (f"lnb{l}", (128, 8), F32, E), (f"lngT{l}", (2, 1024), F32, E),
            (f"qs{l}", (128, 1), F32, E),
            (f"ks{l}", (128, 1), F32, E),
        ]
    for nm, (shape, dt_) in _CONST_SPECS.items():
        specs.append((nm, shape, dt_, E))
    return specs


_FUSED_CACHE = {}


def build_fused():
    if "nc" in _FUSED_CACHE:
        return _FUSED_CACHE["nc"], _FUSED_CACHE["ins"]
    nc = bass.Bass("TRN2", target_bir_lowering=False)
    d = {}
    ins = []
    for nm, shape, dt_, kind in _fused_specs():
        d[nm] = nc.dram_tensor(nm, list(shape), dt_, kind=kind).ap()
        if kind == "ExternalInput":
            ins.append(nm)
    p = Prog(nc)

    def ph(fn):
        p.phase_es = contextlib.ExitStack()
        fn()
        p.barrier()
        p.phase_es.close()
        p.phase_es = None

    emit_p6_stage(p, d)
    ph(lambda: emit_p0(p, d))
    ph(lambda: emit_p1(p, d))
    ph(lambda: emit_ffn(p, d, 0, d["xnz"], False, d["t_y2"]))
    ph(lambda: emit_ffn(p, d, 1, d["t_y2"], True, d["t_y4"]))
    ph(lambda: emit_p4(p, d))
    ph(lambda: emit_p5(p, d))
    ph(lambda: emit_p6(p, d))
    p.finalize()
    _FUSED_CACHE["nc"] = nc
    _FUSED_CACHE["ins"] = ins
    return nc, ins


_PHASE_CACHE = {}


def _phases():
    if not _PHASE_CACHE:
        _PHASE_CACHE["list"] = [
            phase_p0(), phase_p1(), phase_ffn(0), phase_ffn(1),
            phase_p4(), phase_p5(), phase_p6(),
        ]
    return _PHASE_CACHE["list"]


FUSED = True


def kernel(**inputs):
    state = {}
    state.update(host_consts())
    state.update(host_weights(inputs))
    x = np.asarray(inputs["L0_attn_out"], np.float32)
    state["x_fm"] = [np.ascontiguousarray(x[b].T) for b in range(NCORE)]
    if FUSED:
        nc_, ins_ = build_fused()
        _run_phase(nc_, ins_, ["out_fm"], state)
    else:
        for nc_, ins_, outs_ in _phases():
            _run_phase(nc_, ins_, outs_, state)
    return np.stack([np.ascontiguousarray(state["out_fm"][b].T)
                     for b in range(NCORE)])
